# version 20
# baseline (speedup 1.0000x reference)
import os
import numpy as np
import ml_dtypes
import concourse.bass as bass
import concourse.mybir as mybir
from concourse.ap import AP
from concourse.bass_utils import run_bass_kernel_spmd
from concourse.bass_core_rust import IndirectOffsetOnAxis
from contextlib import ExitStack

F32 = mybir.dt.float32
BF16 = mybir.dt.bfloat16
U32 = mybir.dt.uint32
AF = mybir.ActivationFunctionType
ALU = mybir.AluOpType
AX = mybir.AxisListType

NCORE = 8
S, D, NH, HD, LAT = 2048, 1024, 16, 64, 512
NT = S // 128
E, EPC, TOPK, CAP = 64, 8, 8, 512
ISH, IEX = 2048, 512
T_ALL = NCORE * S
NT_ALL = T_ALL // 128
SHIFT = 20.0
bfdt = ml_dtypes.bfloat16
ENGS = ("tensor", "vector", "scalar", "gpsimd", "sync")


def host_consts():
    c = {}
    c["ident_f32"] = np.eye(128, dtype=np.float32)
    c["ident_bf"] = np.eye(128, dtype=bfdt)
    inv_freq = (1.0 / (10000.0 ** (np.arange(0, HD, 2, dtype=np.float32) / HD))).astype(np.float32)
    pos = np.arange(S, dtype=np.float32)
    freqs = pos[:, None] * inv_freq[None, :]
    emb = np.concatenate([freqs, freqs], axis=1)
    cos = np.cos(emb).astype(np.float32)
    sin = np.sin(emb).astype(np.float32)
    sinS = sin.copy()
    sinS[:, :32] *= -1.0
    c["cosQ"] = (cos * 0.125).astype(bfdt)
    c["sinQ"] = (sinS * 0.125).astype(bfdt)
    c["cosK"] = cos.astype(bfdt)
    c["sinK"] = sinS.astype(bfdt)
    p = np.arange(128)[:, None]
    f = np.arange(512)[None, :]
    c["mask01"] = np.ascontiguousarray(
        np.stack([((128 * j + p) <= f) for j in range(4)]).astype(bfdt))
    c["Ustrict"] = (np.arange(128)[:, None] < np.arange(128)[None, :]).astype(bfdt)
    c["iota128"] = np.broadcast_to(np.arange(128, dtype=np.float32), (128, 128)).copy()
    c["iota64"] = np.broadcast_to(np.arange(64, dtype=np.float32), (128, 64)).copy()
    c["TI"] = np.ascontiguousarray(
        np.arange(128, dtype=np.float32)[:, None]
        + 128.0 * np.arange(NT_ALL, dtype=np.float32)[None, :])
    rb = np.zeros(E, np.float32)
    for e in range(E):
        if e < 32:
            rb[e] = (e // 4) * 2048 + (e % 4) * 512
        else:
            rb[e] = 16384 + ((e - 32) // 4) * 2048 + ((e - 32) % 4) * 512
    c["rowbase"] = np.broadcast_to(rb, (128, E)).copy()
    sh = np.zeros((128, 128), dtype=bfdt)
    sh[np.arange(64), np.arange(64) + 64] = 1.0
    c["shift64"] = sh
    return c


def expert_windows(margin_sig=8.0):
    wins = []
    pr = 1.0 / 8.0
    ts = np.arange(T_ALL + 1, dtype=np.float64)
    mean, sig = ts * pr, np.sqrt(ts * pr * (1 - pr))
    for g in range(4):
        if g == 0:
            lo = 0
        else:
            ok = np.nonzero(mean + margin_sig * sig >= 128 * g)[0]
            lo = int(ok[0]) // 128 if len(ok) else 0
        ok = np.nonzero(mean - margin_sig * sig <= 128 * (g + 1))[0]
        hi = min(NT_ALL, int(ok[-1]) // 128 + 1) if len(ok) else NT_ALL
        wins.append((lo, hi))
    return wins


def bc(ap, n, pos=1):
    dims = list(ap.ap)
    dims.insert(pos, [0, n])
    return AP(ap.tensor, ap.offset, dims)


class Sem:
    def __init__(self, es, nc, name):
        self.h = es.enter_context(nc.semaphore(name))
        self.n = 0
        self.name = name


class Builder:
    def __init__(self, debug_taps=(), phases=99):
        self.phases = phases
        nc = self.nc = bass.Bass("TRN2", target_bir_lowering=False)
        self.Ch = host_consts()
        self.q = {e: [] for e in ENGS}
        self.taps = {}
        self.debug_taps = debug_taps

        din = lambda name, shape, dt=F32: nc.dram_tensor(name, shape, dt, kind="ExternalInput")
        self.xT_d = din("xT", [D, S])
        self.x_d = din("x", [S, D])
        self.wqT_d = din("wqT", [D, D])
        self.wkvT_d = din("wkvT", [D, LAT])
        self.wkuT_d = din("wkuT", [LAT, D])
        self.wvuT_d = din("wvuT", [LAT, D])
        self.woT_d = din("woT", [D, D])
        self.sgu_d = din("sgu", [D, 2 * ISH])
        self.sdw_d = din("sdw", [ISH, D])
        self.sgt_d = din("sgt", [D, 1])
        self.gw_d = din("gw", [D, E])
        self.egu_d = din("egu", [EPC, D, 2 * IEX])
        self.edw_d = din("edw", [EPC, IEX, D])
        self.sel8_d = din("sel8", [E, EPC])
        self.prior_d = din("prior", [E, 128])

        self.out_d = nc.dram_tensor("out", [S, D], F32, kind="ExternalOutput")

        self.cc_x1_in = nc.dram_tensor("cc_x1_in", [S, D], BF16)
        self.cc_x1_out = nc.dram_tensor("cc_x1_out", [T_ALL, D], BF16)
        self.cc_at_in = nc.dram_tensor("cc_at_in", [E, S], F32)
        self.cc_at_out = nc.dram_tensor("cc_at_out", [NCORE * E, S], F32)
        self.cc_ob_in = nc.dram_tensor("cc_ob_in", [EPC * CAP, D], BF16)
        self.cc_ob_out = nc.dram_tensor("cc_ob_out", [NCORE * EPC * CAP, D], BF16)
        self.x1f_dram = nc.dram_tensor("x1f_dram", [S, D], F32)
        self.shf_dram = nc.dram_tensor("shf_dram", [S, D], F32)

        ic = nc.inline_tensor
        self.ident_c = ic(self.Ch["ident_f32"], name="ident_c")
        self.identbf_c = ic(self.Ch["ident_bf"], name="identbf_c")
        self.cosQ_c = ic(self.Ch["cosQ"], name="cosQ_c")
        self.sinQ_c = ic(self.Ch["sinQ"], name="sinQ_c")
        self.cosK_c = ic(self.Ch["cosK"], name="cosK_c")
        self.sinK_c = ic(self.Ch["sinK"], name="sinK_c")
        self.mask01_c = ic(self.Ch["mask01"], name="mask01_c")
        self.Ustrict_c = ic(self.Ch["Ustrict"], name="Ustrict_c")
        self.iota128_c = ic(self.Ch["iota128"], name="iota128_c")
        self.iota64_c = ic(self.Ch["iota64"], name="iota64_c")
        self.TI_c = ic(self.Ch["TI"], name="TI_c")
        self.shift64_c = ic(self.Ch["shift64"], name="shift64_c")
        self.rowbase_c = ic(self.Ch["rowbase"], name="rowbase_c")

        esg = self.esg = ExitStack()
        self.PS_BIG = esg.enter_context(nc.psum_tensor("PS_BIG", [128, 2048], F32))
        self.PS_A = esg.enter_context(nc.psum_tensor("PS_A", [128, 1024], F32))
        self.PS_BF = esg.enter_context(nc.psum_tensor("PS_BF", [128, 2048], BF16))
        for s in ("LD", "SW", "PE", "DV", "AC", "GP", "CCS", "ST",
                  "SWR", "SWGA", "SWGB", "SWDA", "SWDB", "SWE",
                  "SWFA", "SWFB", "SWHA", "SWHB", "LDA", "LDB", "STA", "STB"):
            setattr(self, s, Sem(esg, nc, s))

        esl = self.esl = ExitStack()
        L = lambda name, shape, dt=F32: esl.enter_context(
            nc.sbuf_tensor(name, shape, dt, side="left"))
        self.IDENT = L("IDENT", [128, 128], F32)
        self.IDENTBF = L("IDENTBF", [128, 128], BF16)
        self.SHIFT64 = L("SHIFT64", [128, 128], BF16)
        self.ONES = L("ONES", [128, 128], BF16)
        self.ONEF = L("ONEF", [128, 128], F32)
        self.MASK01 = L("MASK01", [128, 4 * 512], BF16)
        self.W1B = L("W1B", [128, 8 * 1024], BF16)
        self.W2B = L("W2B", [128, 4 * 1024], BF16)
        self.RS1 = L("RS1", [128, 1024], F32)
        self.RS2 = L("RS2", [128, 1024], F32)
        self.RS3 = L("RS3", [128, 1024], F32)
        self.ZROW = L("ZROW", [128, 1024], F32)
        self.NEGSH = L("NEGSH", [128, 1], F32)
        esqkv = self.esqkv = ExitStack()
        Lq = lambda name, shape, dt: esqkv.enter_context(
            nc.sbuf_tensor(name, shape, dt, side="left"))
        self.COSQ = Lq("COSQ", [128, NT * HD], BF16)
        self.SINQ = Lq("SINQ", [128, NT * HD], BF16)
        self.COSK = Lq("COSK", [128, NT * HD], BF16)
        self.SINK = Lq("SINK", [128, NT * HD], BF16)
        self.QR = Lq("QR", [128, NT * 1024], BF16)
        self.KR = Lq("KR", [128, NT * 1024], BF16)
        self.V = Lq("V", [128, NT * NH * 65], BF16)

        esr = self.esr = ExitStack()
        self.XT = esr.enter_context(nc.sbuf_tensor("XT", [128, 8 * 2048], BF16, side="right"))
        self.KVT = esr.enter_context(nc.sbuf_tensor("KVT", [128, 4 * 2048], BF16, side="right"))

    def tap(self, name, shape, dt=F32):
        if name in self.debug_taps and name not in self.taps:
            self.taps[name] = self.nc.dram_tensor("tap_" + name, shape, dt, kind="ExternalOutput")
        return self.taps.get(name)

    def op(self, eng, meth, *args, waits=(), incs=(), **kwargs):
        waits = tuple(waits)
        inc2 = []
        for s, kk in incs:
            s.n += kk
            inc2.append((s, kk))

        def emit(E, meth=meth, args=args, kwargs=kwargs, waits=waits, incs=tuple(inc2)):
            for s, v in waits:
                E.wait_ge(s.h, v)
            inst = getattr(E, meth)(*args, **kwargs)
            for s, kk in incs:
                inst.then_inc(s.h, kk)
            return inst
        self.q[eng].append(emit)

    def T(self, *a, **kw): self.op("tensor", *a, **kw)
    def Vx(self, *a, **kw): self.op("vector", *a, **kw)
    def A(self, *a, **kw): self.op("scalar", *a, **kw)
    def G(self, *a, **kw): self.op("gpsimd", *a, **kw)
    def Y(self, *a, **kw): self.op("sync", *a, **kw)

    def emit_all(self):
        with self.nc.Block("main") as blk:
            for eng in ENGS:
                fns = self.q[eng]
                if fns:
                    getattr(blk, eng)(lambda En, fns=fns: [f(En) for f in fns])

    @staticmethod
    def tl(buf, i, w):
        return buf[:, i * w:(i + 1) * w]

    def rope(self, t, psum, cosb, sinb, dstbuf):
        AC, DV, PE = self.AC, self.DV, self.PE
        self.A("copy", self.RS1[:], psum, waits=[(PE, PE.n), (DV, DV.n)], incs=[(AC, 1)])
        ac_evac = AC.n
        r1 = self.RS1[:].rearrange("p (h half d) -> p h half d", h=NH, half=2, d=32)
        r2 = self.RS2[:].rearrange("p (h half d) -> p h half d", h=NH, half=2, d=32)
        sv = self.tl(sinb, t, HD).rearrange("p (half d) -> p half d", half=2)
        self.Vx("tensor_tensor", r2[:, :, 0, :], r1[:, :, 1, :], bc(sv[:, 0, :], NH),
                ALU.mult, waits=[(AC, AC.n)], incs=[(DV, 1)])
        self.Vx("tensor_tensor", r2[:, :, 1, :], r1[:, :, 0, :], bc(sv[:, 1, :], NH),
                ALU.mult, incs=[(DV, 1)])
        self.Vx("tensor_tensor", self.RS3[:].rearrange("p (h d) -> p h d", d=HD),
                self.RS1[:].rearrange("p (h d) -> p h d", d=HD),
                bc(self.tl(cosb, t, HD), NH), ALU.mult, incs=[(DV, 1)])
        self.Vx("tensor_add", self.tl(dstbuf, t, 1024), self.RS3[:], self.RS2[:],
                waits=[(DV, DV.n)], incs=[(DV, 1)])
        return ac_evac, DV.n

    def phase_a0(self):
        G, Y, LD, SW, GP = self.G, self.Y, self.LD, self.SW, self.GP
        Y("dma_start", out=self.IDENT[:], in_=self.ident_c.ap(), incs=[(LD, 16)])
        for dst, src in ((self.COSQ, self.cosQ_c), (self.SINQ, self.sinQ_c),
                         (self.COSK, self.cosK_c), (self.SINK, self.sinK_c)):
            G("dma_start", out=dst[:].rearrange("p (t d) -> p t d", d=HD),
              in_=src.ap().rearrange("(t p) d -> p t d", p=128), incs=[(SW, 16)])
        G("dma_start", out=self.IDENTBF[:], in_=self.identbf_c.ap(), incs=[(SW, 16)])
        G("dma_start", out=self.SHIFT64[:], in_=self.shift64_c.ap(), incs=[(SW, 16)])
        G("dma_start", out=self.MASK01[:].rearrange("p (j f) -> p j f", f=512),
          in_=self.mask01_c.ap().rearrange("j p f -> p j f"), incs=[(SW, 16)])
        G("dma_start", out=self.XT[:].rearrange("p (c t) -> p c t", t=S),
          in_=self.xT_d.ap().rearrange("(c p) t -> p c t", p=128), incs=[(SW, 16)])
        G("dma_start", out=self.W1B[:].rearrange("p (c n) -> p c n", n=1024),
          in_=self.wqT_d.ap().rearrange("(c p) n -> p c n", p=128), incs=[(SW, 16)])
        G("dma_start", out=self.W2B[:].rearrange("p (c n) -> p c n", n=LAT),
          in_=self.wkvT_d.ap().rearrange("(c p) n -> p c n", p=128), incs=[(SW, 16)])
        G("memset", self.ONES[:], 1.0, incs=[(GP, 1)])
        G("memset", self.NEGSH[:], -SHIFT, incs=[(GP, 1)])
        self.ld_a0 = self.LD.n
        G("memset", self.ONEF[:], 1.0, incs=[(GP, 1)])
        vv = self.V[:].rearrange("p (t h c) -> p t h c", h=NH, c=65)
        G("memset", vv[:, :, :, 64], 1.0, incs=[(GP, 1)])
        self.sw_a0, self.gp_a0 = self.SW.n, self.GP.n

    def phase_a1(self):
        T, Vx, PE, DV, SW, GP = self.T, self.Vx, self.PE, self.DV, self.SW, self.GP
        self.q_rope_done = []
        for t in range(NT):
            half = t % 2
            ps = self.PS_BIG[:, half * 1024:half * 1024 + 1024]
            w = [(SW, self.sw_a0), (GP, self.gp_a0), (self.LD, self.ld_a0)] if t == 0 else []
            if t >= 2:
                w = [(self.AC, self.q_rope_done[t - 2][0])]
            for c in range(8):
                for nh2 in range(2):
                    T("matmul", ps[:, nh2 * 512:nh2 * 512 + 512],
                      self.XT[:, c * 2048 + t * 128: c * 2048 + t * 128 + 128],
                      self.W1B[:, c * 1024 + nh2 * 512: c * 1024 + nh2 * 512 + 512],
                      start=(c == 0), stop=(c == 7),
                      waits=w if (c == 0 and nh2 == 0) else (),
                      incs=[(PE, 1)] if (c == 7 and nh2 == 1) else ())
            self.q_rope_done.append(self.rope(t, ps, self.COSQ, self.SINQ, self.QR))
        self.q_done_pe = PE.n

        for m in range(4):
            for j in range(4):
                w = [(DV, DV.n)] if (m, j) != (0, 0) else []
                for c in range(8):
                    T("matmul", self.PS_A[:, 0:512],
                      self.W2B[:, c * 512 + m * 128: c * 512 + m * 128 + 128],
                      self.XT[:, c * 2048 + j * 512: c * 2048 + j * 512 + 512],
                      start=(c == 0), stop=(c == 7),
                      waits=w if c == 0 else (),
                      incs=[(PE, 1)] if c == 7 else ())
                Vx("tensor_copy", self.KVT[:, m * 2048 + j * 512: m * 2048 + j * 512 + 512],
                   self.PS_A[:, 0:512], waits=[(PE, PE.n)], incs=[(DV, 1)])
        self.kv_done_dv = DV.n

    def phase_a2(self):
        T, Vx, G = self.T, self.Vx, self.G
        PE, DV, SW = self.PE, self.DV, self.SW
        w1v = self.W1B[:].rearrange("p (c n) -> p c n", n=1024)
        G("dma_start", out=w1v[:, 0:4, :],
          in_=self.wkuT_d.ap().rearrange("(c p) n -> p c n", p=128),
          waits=[(PE, self.q_done_pe)], incs=[(SW, 16)])
        G("dma_start", out=w1v[:, 4:8, :],
          in_=self.wvuT_d.ap().rearrange("(c p) n -> p c n", p=128), incs=[(SW, 16)])
        sw_ku = SW.n
        k_prev = None
        for t in range(NT):
            wk = [(SW, sw_ku), (DV, self.kv_done_dv)] if t == 0 else []
            if k_prev is not None:
                wk = [(self.AC, k_prev[0])]
            for c in range(4):
                for nh2 in range(2):
                    T("matmul", self.PS_BIG[:, nh2 * 512:nh2 * 512 + 512],
                      self.KVT[:, c * 2048 + t * 128: c * 2048 + t * 128 + 128],
                      self.W1B[:, c * 1024 + nh2 * 512: c * 1024 + nh2 * 512 + 512],
                      start=(c == 0), stop=(c == 3),
                      waits=wk if (c == 0 and nh2 == 0) else (),
                      incs=[(PE, 1)] if (c == 3 and nh2 == 1) else ())
            wv = [(DV, k_prev[1])] if k_prev is not None else []
            for c in range(4):
                for nh2 in range(2):
                    T("matmul", self.PS_BIG[:, 1024 + nh2 * 512:1024 + nh2 * 512 + 512],
                      self.KVT[:, c * 2048 + t * 128: c * 2048 + t * 128 + 128],
                      self.W1B[:, 4096 + c * 1024 + nh2 * 512: 4096 + c * 1024 + nh2 * 512 + 512],
                      start=(c == 0), stop=(c == 3),
                      waits=wv if (c == 0 and nh2 == 0) else (),
                      incs=[(PE, 1)] if (c == 3 and nh2 == 1) else ())
            vpe = PE.n
            k_ac, _ = self.rope(t, self.PS_BIG[:, 0:1024], self.COSK, self.SINK, self.KR)
            vdst = self.V[:].rearrange("p (tt h c) -> p tt h c", h=NH, c=65)[:, t, :, 0:64]
            Vx("tensor_copy", vdst, self.PS_BIG[:, 1024:2048].rearrange("p (h d) -> p h d", d=HD),
               waits=[(PE, vpe)], incs=[(DV, 1)])
            k_prev = (k_ac, DV.n)
        self.proj_done_dv = DV.n
        self.proj_done_pe = PE.n
        self.esr.close()

    def phase_attn(self):
        nc, T, Vx, A = self.nc, self.T, self.Vx, self.A
        PE, DV, AC = self.PE, self.DV, self.AC
        esr = self.esr = ExitStack()
        R = lambda name, shape, dt: esr.enter_context(
            nc.sbuf_tensor(name, shape, dt, side="right"))
        self.ATO = R("ATO", [128, 8 * 2048], BF16)
        esr2 = self.esr2 = ExitStack()
        R2 = lambda name, shape, dt: esr2.enter_context(
            nc.sbuf_tensor(name, shape, dt, side="right"))
        PTa = R2("PTa", [128, 1024], BF16)
        PTb = R2("PTb", [128, 1024], BF16)
        QRT = R2("QRT", [128, 2 * 2048], BF16)
        KRT = R2("KRT", [128, 2 * 2048], BF16)
        ZB = R2("ZB", [64, 1024], F32)
        NORMB = R2("NORMB", [128, 512], BF16)

        pv_consumed_dv = 0
        arena_free_ac = {0: 0, 1: 0}
        pt_free_pe = {}
        mask_dv = {}
        arena_war = {0: 0}
        tr_state = {}
        tr_done = {}

        def emit_tr(p2n, quarter):
            parn = p2n % 2
            ev = tr_state.setdefault(p2n, [])
            for bi, (srcbuf, dstbuf) in enumerate(((self.QR, QRT), (self.KR, KRT))):
                for t in range(NT):
                    jj = bi * NT + t
                    if jj // 8 != quarter:
                        continue
                    pbf = self.PS_BF[:, (jj % 2) * 1024:(jj % 2) * 1024 + 128]
                    w = []
                    if p2n == 0 and jj == 0:
                        w = [(DV, self.proj_done_dv)]
                    if len(ev) >= 2:
                        w.append((DV, ev[-2]))
                    T("transpose", pbf,
                      srcbuf[:, t * 1024 + p2n * 128: t * 1024 + p2n * 128 + 128],
                      self.IDENTBF[:], waits=w, incs=[(PE, 1)])
                    Vx("tensor_copy",
                       dstbuf[:, parn * 2048 + t * 128: parn * 2048 + (t + 1) * 128],
                       pbf, waits=[(PE, PE.n)], incs=[(DV, 1)])
                    ev.append(DV.n)
            if quarter == 3:
                tr_done[p2n] = DV.n

        for q in range(4):
            emit_tr(0, q)
        for p2 in range(8):
            ha, hb = 2 * p2, 2 * p2 + 1
            par2 = p2 % 2
            first_st = True
            for qb in range(4):
                nkt = 4 * (qb + 1)
                qc = slice(par2 * 2048 + qb * 512, par2 * 2048 + qb * 512 + 512)
                outp = (self.PS_A[0:65, 0:512], self.PS_A[0:65, 512:1024])
                exp_ac = {0: 0, 1: 0}
                for ktp in range(nkt // 2):
                    par = 0
                    for hi, base in ((0, 0), (1, 64)):
                        PT = PTa if hi == 0 else PTb
                        PTp = PT[:, 0:1024]
                        arena = self.PS_BIG[:, hi * 1024:(hi + 1) * 1024]
                        wv = []
                        if first_st:
                            wv.append((DV, tr_done[p2]))
                            first_st = False
                        if arena_free_ac[hi]:
                            wv.append((AC, arena_free_ac[hi]))
                        if hi == 0 and ktp == 0 and arena_war[0]:
                            wv.append((DV, arena_war[0]))
                        for kk in range(2):
                            kt = 2 * ktp + kk
                            T("matmul", arena[:, kk * 512:kk * 512 + 512],
                              KRT[base:base + 64,
                                  par2 * 2048 + kt * 128: par2 * 2048 + (kt + 1) * 128],
                              QRT[base:base + 64, qc],
                              start=True, stop=True, tile_position=(base, 0),
                              waits=wv if kk == 0 else (),
                              incs=[(PE, 1)] if kk == 1 else ())
                        st_pe = PE.n
                        wexp = [(PE, st_pe)]
                        if (hi, par) in pt_free_pe:
                            wexp.append((PE, pt_free_pe[(hi, par)]))
                        if (hi, par) in mask_dv:
                            wexp.append((DV, mask_dv[(hi, par)]))
                        A("activation", PTp[:, 0:1024], arena[:, 0:1024], AF.Exp,
                          bias=self.NEGSH[:, 0:1], scale=1.0, waits=wexp, incs=[(AC, 1)])
                        exp_ac[hi] = arena_free_ac[hi] = AC.n
                        for kk in range(2):
                            kt = 2 * ktp + kk
                            jj = kt - 4 * qb
                            if 0 <= jj <= 3:
                                Vx("tensor_tensor", PTp[:, kk * 512:(kk + 1) * 512],
                                   PTp[:, kk * 512:(kk + 1) * 512],
                                   self.MASK01[:, jj * 512:(jj + 1) * 512],
                                   ALU.mult, waits=[(AC, AC.n)], incs=[(DV, 1)])
                                mask_dv[(hi, par)] = DV.n
                    for hi in (0, 1):
                        PT = PTa if hi == 0 else PTb
                        PTp = PT[:, 0:1024]
                        h = ha if hi == 0 else hb
                        wpv = [(AC, exp_ac[hi])]
                        if (hi, par) in mask_dv:
                            wpv.append((DV, mask_dv[(hi, par)]))
                        if ktp == 0 and pv_consumed_dv:
                            wpv.append((DV, pv_consumed_dv))
                        for kk in range(2):
                            kt = 2 * ktp + kk
                            vsl = self.V[:, kt * NH * 65 + h * 65: kt * NH * 65 + h * 65 + 65]
                            T("matmul", outp[hi], vsl, PTp[:, kk * 512:(kk + 1) * 512],
                              start=(ktp == 0 and kk == 0),
                              stop=(ktp == nkt // 2 - 1 and kk == 1),
                              waits=wpv if kk == 0 else (),
                              incs=[(PE, 1)] if kk == 1 else ())
                        pt_free_pe[(hi, par)] = PE.n
                pv_done_pe = PE.n
                A("activation", self.ZROW[64:65, 0:1024], self.PS_A[64:65, 0:1024],
                  AF.Copy, waits=[(PE, pv_done_pe)], incs=[(AC, 1)])
                T("matmul", self.PS_BIG[0:64, 0:512], self.ONEF[64:65, 0:64],
                  self.ZROW[64:65, 0:512], start=True, stop=True,
                  waits=[(AC, AC.n), (AC, arena_free_ac[0])], incs=[(PE, 1)])
                T("matmul", self.PS_BIG[0:64, 512:1024], self.ONEF[64:65, 0:64],
                  self.ZROW[64:65, 512:1024], start=True, stop=True, incs=[(PE, 1)])
                Vx("reciprocal", ZB[0:64, 0:512], self.PS_BIG[0:64, 0:512],
                   waits=[(PE, PE.n)], incs=[(DV, 1)])
                zb_a = DV.n
                Vx("reciprocal", ZB[0:64, 512:1024],
                   self.PS_BIG[0:64, 512:1024], incs=[(DV, 1)])
                zb_b = DV.n
                ocol = p2 * 2048 + qb * 512
                Vx("tensor_tensor", self.ATO[0:64, ocol:ocol + 512],
                   self.PS_A[0:64, 0:512], ZB[0:64, 0:512], ALU.mult,
                   waits=[(DV, zb_a)], incs=[(DV, 1)])
                Vx("tensor_tensor", NORMB[0:64, 0:512],
                   self.PS_A[0:64, 512:1024], ZB[0:64, 512:1024], ALU.mult,
                   waits=[(DV, zb_b)], incs=[(DV, 1)])
                pv_consumed_dv = DV.n
                T("matmul", self.PS_BIG[:, 0:512], self.SHIFT64[0:64, :],
                  NORMB[0:64, 0:512], start=True, stop=True,
                  waits=[(DV, DV.n)], incs=[(PE, 1)])
                Vx("tensor_copy", self.ATO[64:128, ocol:ocol + 512],
                   self.PS_BIG[64:128, 0:512], waits=[(PE, PE.n)], incs=[(DV, 1)])
                arena_war[0] = DV.n
                if p2 < 7:
                    emit_tr(p2 + 1, qb)
        self.attn_done_dv = DV.n
        esr2.close()

    def phase_wo_ln(self):
        nc, T, Vx, A, G, Y = self.nc, self.T, self.Vx, self.A, self.G, self.Y
        PE, DV, AC, SW, LD, ST = self.PE, self.DV, self.AC, self.SW, self.LD, self.ST
        self.esqkv.close()
        self.X1T = self.esl.enter_context(
            nc.sbuf_tensor("X1T", [128, 8 * 2048], BF16, side="left"))
        esr2 = ExitStack()
        R2 = lambda name, shape, dt: esr2.enter_context(
            nc.sbuf_tensor(name, shape, dt, side="right"))
        XF = R2("XF", [128, 2 * 1024], F32)
        XA = R2("XA", [128, 4 * 1024], F32)
        MV = R2("MV", [128, 16], F32)

        w1v = self.W1B[:].rearrange("p (c n) -> p c n", n=1024)
        G("dma_start", out=w1v[:, :, :],
          in_=self.woT_d.ap().rearrange("(c p) n -> p c n", p=128),
          waits=[(PE, self.PE.n)], incs=[(SW, 16)])
        sw_wo = SW.n

        x1_tap = self.tap("x1", [S, D])
        xa_free = None
        for tb in range(4):
            xa_ready = []
            for ch in range(8):
                ps = self.PS_BIG[:, (ch % 2) * 512:(ch % 2) * 512 + 512]
                w = [(SW, sw_wo), (DV, self.attn_done_dv)] if (tb == 0 and ch == 0) else []
                if tb > 0 or ch >= 2:
                    w = list(w) + [(AC, AC.n)]
                for kq in range(8):
                    T("matmul", ps,
                      self.W1B[:, kq * 1024 + ch * 128: kq * 1024 + ch * 128 + 128],
                      self.ATO[:, kq * 2048 + tb * 512: kq * 2048 + tb * 512 + 512],
                      start=(kq == 0), stop=(kq == 7),
                      waits=w if kq == 0 else (), incs=[(PE, 1)])
                A("copy", self.RS1[:, 0:512], ps, waits=[(PE, PE.n), (DV, DV.n)], incs=[(AC, 1)])
                ac_ev = AC.n
                for j in range(4):
                    pslot = (self.PS_BIG[:, 1024:1152] if j % 2 == 0
                             else self.PS_BIG[:, 1536:1664])
                    T("transpose", pslot, self.RS1[:, j * 128:(j + 1) * 128], self.IDENT[:],
                      waits=[(AC, ac_ev), (DV, DV.n)], incs=[(PE, 1)])
                    wxa = [(PE, PE.n)]
                    if ch == 0 and j == 0 and xa_free is not None:
                        wxa += [(SW, xa_free[0]), (self.STB, xa_free[1])]
                    Vx("tensor_copy", XA[:, j * 1024 + ch * 128: j * 1024 + ch * 128 + 128],
                       pslot, waits=wxa, incs=[(DV, 1)])
                xa_ready.append(DV.n)
            for j in range(4):
                t = tb * 4 + j
                xf = XF[:, (j % 2) * 1024:(j % 2) * 1024 + 1024]
                LDP = self.LDA if j % 2 == 0 else self.LDB
                Y("dma_start", out=xf, in_=self.x_d.ap()[t * 128:(t + 1) * 128, :],
                  waits=[(DV, DV.n)], incs=[(LDP, 16)])
                xa = XA[:, j * 1024:(j + 1) * 1024]
                Vx("tensor_add", xa, xa, xf, waits=[(LDP, LDP.n), (DV, xa_ready[-1])], incs=[(DV, 1)])
                Vx("bn_stats", MV[:, 0:6], xa[:, 0:512], waits=[(DV, DV.n)], incs=[(DV, 1)])
                Vx("bn_stats", MV[:, 6:12], xa[:, 512:1024], incs=[(DV, 1)])
                Vx("bn_aggr", MV[:, 12:14], MV[:, 0:12], waits=[(DV, DV.n)], incs=[(DV, 1)])
                Vx("tensor_scalar", MV[:, 14:15], MV[:, 13:14], 1e-5, None, ALU.add,
                   waits=[(DV, DV.n)], incs=[(DV, 1)])
                A("sqrt", MV[:, 15:16], MV[:, 14:15], waits=[(DV, DV.n)], incs=[(AC, 1)])
                Vx("reciprocal", MV[:, 10:11], MV[:, 15:16], waits=[(AC, AC.n)], incs=[(DV, 1)])
                Vx("tensor_scalar", xa, xa, MV[:, 12:13], MV[:, 10:11],
                   ALU.subtract, ALU.mult, waits=[(DV, DV.n)], incs=[(DV, 1)])
                x1dv = DV.n
                Y("dma_start", out=self.x1f_dram.ap()[t * 128:(t + 1) * 128, :], in_=xa,
                  waits=[(DV, x1dv)], incs=[(self.STB, 16)])
                if x1_tap is not None:
                    Y("dma_start", out=x1_tap.ap()[t * 128:(t + 1) * 128, :], in_=xa,
                      incs=[(ST, 16)])
                G("dma_start", out=self.cc_x1_in.ap()[t * 128:(t + 1) * 128, :], in_=xa,
                  waits=[(DV, x1dv)], incs=[(SW, 16)])
                for cch in range(8):
                    pslot = (self.PS_BIG[:, 1024:1152] if cch % 2 == 0
                             else self.PS_BIG[:, 1536:1664])
                    T("transpose", pslot, xa[:, cch * 128:(cch + 1) * 128], self.IDENT[:],
                      waits=[(DV, x1dv)] if cch == 0 else [(DV, DV.n)], incs=[(PE, 1)])
                    Vx("tensor_copy", self.X1T[:, cch * 2048 + t * 128: cch * 2048 + t * 128 + 128],
                       pslot, waits=[(PE, PE.n)], incs=[(DV, 1)])
            xa_free = (SW.n, self.STB.n)
        self.x1_done_dv = DV.n
        self.x1_store_sw = SW.n
        self.x1f_stb = self.STB.n
        esr2.close()
        self.esr.close()

    def finish_debug(self, phases):
        Y, ST, SW, DV, PE = self.Y, self.ST, self.SW, self.DV, self.PE
        G = self.G
        t_qr = self.tap("qr", [128, NT * 1024], BF16)
        if t_qr is not None and phases >= 1:
            G("dma_start", out=t_qr.ap(), in_=self.QR[:],
              waits=[(DV, DV.n)], incs=[(SW, 16)])
        t_kr = self.tap("kr", [128, NT * 1024], BF16)
        t_v = self.tap("v", [128, NT * NH * 65], BF16)
        if phases >= 2:
            if t_kr is not None:
                G("dma_start", out=t_kr.ap(), in_=self.KR[:],
                  waits=[(DV, DV.n)], incs=[(SW, 16)])
            if t_v is not None:
                G("dma_start", out=t_v.ap(), in_=self.V[:],
                  waits=[(DV, DV.n)], incs=[(SW, 16)])
        if phases >= 3:
            t_ato = self.tap("ato", [128, 8 * 2048], BF16)
            if t_ato is not None:
                G("dma_start", out=t_ato.ap(), in_=self.ATO[:],
                  waits=[(DV, DV.n)], incs=[(SW, 16)])
        Y("dma_start", out=self.out_d.ap(), in_=self.x1f_dram.ap(),
          waits=[(ST, ST.n)], incs=[(ST, 16)])
        Y("nop", waits=[(ST, ST.n), (SW, SW.n)])

    def finish_stub(self):
        Y, ST, SW = self.Y, self.ST, self.SW
        Y("dma_start", out=self.out_d.ap(), in_=self.x1f_dram.ap(),
          waits=[(ST, ST.n), (SW, SW.n)], incs=[(ST, 16)])
        Y("nop", waits=[(ST, ST.n), (SW, SW.n)])

    def phase_coll1(self):
        self.G("collective_compute", "AllGather", ALU.bypass,
               replica_groups=[list(range(NCORE))],
               ins=[self.cc_x1_in.ap()], outs=[self.cc_x1_out.ap()],
               waits=[(self.SW, self.x1_store_sw)], incs=[(self.CCS, 1)])
        self.cc1 = self.CCS.n

    def phase_router(self):
        nc, T, Vx, A, G, Y = self.nc, self.T, self.Vx, self.A, self.G, self.Y
        PE, DV, AC, SW, ST = self.PE, self.DV, self.AC, self.SW, self.ST
        esr = self.esrR = ExitStack()
        R = lambda name, shape, dt=F32: esr.enter_context(
            nc.sbuf_tensor(name, shape, dt, side="right"))
        GWB = R("GWB", [128, 8 * 64], BF16)
        IOTA64B = R("IOTA64B", [128, 64], F32)
        IOTA128B = R("IOTA128B", [128, 128], F32)
        TIB = R("TIB", [128, NT_ALL], F32)
        UB = R("UB", [128, 128], BF16)
        SEL8B = R("SEL8B", [128, EPC], BF16)
        PRIORB = R("PRIORB", [128, 128], F32)
        LOG = R("LOG", [128, 64], F32)
        TOP8 = R("TOP8", [128, 8], F32)
        IDXU = R("IDXU", [128, 8], U32)
        NEG = R("NEG", [128, 1], F32)
        E8 = R("E8", [128, 8], F32)
        DEN = R("DEN", [128, 2], F32)
        A0a = R("A0a", [128, 64], F32)
        A0b = R("A0b", [128, 64], F32)
        MK = R("MK", [128, 64], F32)
        A0T = R("A0T", [64, S], F32)
        ROWB = R("ROWB", [128, E], F32)
        self.ROWB = ROWB
        esl = self.esl
        Lf = lambda name, shape, dt: esl.enter_context(
            nc.sbuf_tensor(name, shape, dt, side="left"))
        self.IDXF = Lf("IDXF", [128, NT * 8], F32)
        self.A8 = Lf("A8", [128, NT * 8], F32)
        self.M_OWN = Lf("M_OWN", [128, NT * 64], BF16)
        self.SLOTS_OWN = Lf("SLOTS_OWN", [128, NT * 64], F32)
        self.FU = Lf("FU", [128, NT * 8], U32)
        self.VALID = Lf("VALID", [128, NT * 8], F32)
        self.TSS = Lf("TSS", [64, 128], F32)
        self.PROW = Lf("PROW", [128, 64], F32)

        SWR = self.SWR
        G("nop", waits=[(self.STB, self.x1f_stb), (SW, self.x1_store_sw),
                        (self.ST, self.ST.n)])
        G("dma_start", out=GWB[:].rearrange("p (c n) -> p c n", n=64),
          in_=self.gw_d.ap().rearrange("(c p) n -> p c n", p=128), incs=[(SWR, 16)])
        G("dma_start", out=IOTA64B[:], in_=self.iota64_c.ap(), incs=[(SWR, 16)])
        G("dma_start", out=IOTA128B[:], in_=self.iota128_c.ap(), incs=[(SWR, 16)])
        G("dma_start", out=TIB[:], in_=self.TI_c.ap(), incs=[(SWR, 16)])
        self.tib_swr = SWR.n
        G("dma_start", out=UB[:], in_=self.Ustrict_c.ap(), incs=[(SWR, 16)])
        G("dma_start", out=SEL8B[0:64, :], in_=self.sel8_d.ap(), incs=[(SWR, 16)])
        G("dma_start", out=PRIORB[0:64, :], in_=self.prior_d.ap(), incs=[(SWR, 16)])
        G("dma_start", out=ROWB[:], in_=self.rowbase_c.ap(), incs=[(SWR, 16)])
        sw_r = SWR.n

        for t in range(NT):
            w = [(SWR, sw_r), (DV, self.x1_done_dv)] if t == 0 else [(DV, DV.n)]
            for c in range(8):
                T("matmul", self.PS_A[:, 0:64],
                  self.X1T[:, c * 2048 + t * 128: c * 2048 + t * 128 + 128],
                  GWB[:, c * 64:(c + 1) * 64],
                  start=(c == 0), stop=(c == 7),
                  waits=w if c == 0 else (), incs=[(PE, 1)])
            Vx("tensor_copy", LOG[:], self.PS_A[:, 0:64], waits=[(PE, PE.n)], incs=[(DV, 1)])
            Vx("max", TOP8[:], LOG[:], waits=[(DV, DV.n)], incs=[(DV, 1)])
            Vx("max_index", IDXU[:], TOP8[:], LOG[:], waits=[(DV, DV.n)], incs=[(DV, 1)])
            Vx("tensor_copy", self.IDXF[:, t * 8:(t + 1) * 8], IDXU[:],
               waits=[(DV, DV.n)], incs=[(DV, 1)])
            Vx("tensor_scalar_mul", NEG[:], TOP8[:, 0:1], -1.0, incs=[(DV, 1)])
            A("activation", E8[:], TOP8[:], AF.Exp, bias=NEG[:, 0:1], scale=1.0,
              waits=[(DV, DV.n)], incs=[(AC, 1)])
            Vx("reduce_sum", DEN[:, 0:1], E8[:], axis=AX.X, waits=[(AC, AC.n)], incs=[(DV, 1)])
            Vx("reciprocal", DEN[:, 1:2], DEN[:, 0:1], waits=[(DV, DV.n)], incs=[(DV, 1)])
            Vx("tensor_scalar_mul", self.A8[:, t * 8:(t + 1) * 8], E8[:], DEN[:, 1:2],
               waits=[(DV, DV.n)], incs=[(DV, 1)])
            Vx("memset", A0a[:], 0.0, waits=[(PE, PE.n)], incs=[(DV, 1)])
            cur, nxt = A0a, A0b
            for kk in range(8):
                Vx("tensor_scalar", MK[:], IOTA64B[:], self.IDXF[:, t * 8 + kk: t * 8 + kk + 1],
                   None, ALU.is_equal, waits=[(DV, DV.n)], incs=[(DV, 1)])
                Vx("scalar_tensor_tensor", nxt[:], MK[:], self.A8[:, t * 8 + kk: t * 8 + kk + 1],
                   cur[:], ALU.mult, ALU.add, waits=[(DV, DV.n)], incs=[(DV, 1)])
                cur, nxt = nxt, cur
            Vx("tensor_single_scalar", self.M_OWN[:, t * 64:(t + 1) * 64], cur[:], 0.0,
               ALU.not_equal, waits=[(DV, DV.n)], incs=[(DV, 1)])
            T("transpose", self.PS_BIG[0:64, 1024:1152], cur[:],
              self.IDENT[:], waits=[(DV, DV.n), (AC, AC.n)], incs=[(PE, 1)])
            Vx("tensor_copy", A0T[0:64, t * 128:(t + 1) * 128], self.PS_BIG[0:64, 1024:1152],
               waits=[(PE, PE.n)], incs=[(DV, 1)])
        Y("dma_start", out=self.cc_at_in.ap(), in_=A0T[0:64, :],
          waits=[(DV, DV.n)], incs=[(self.STA, 16)])
        self.G("collective_compute", "AllGather", ALU.bypass,
               replica_groups=[list(range(NCORE))],
               ins=[self.cc_at_in.ap()], outs=[self.cc_at_out.ap()],
               waits=[(self.STA, self.STA.n)], incs=[(self.CCS, 1)])
        self.cc2 = self.CCS.n
        self.router_dv = DV.n
        self.GWB, self.IOTA64B, self.IOTA128B = GWB, IOTA64B, IOTA128B
        self.TIB, self.UB, self.SEL8B, self.PRIORB = TIB, UB, SEL8B, PRIORB

    def phase_shared(self):
        nc, T, Vx, A, G, Y = self.nc, self.T, self.Vx, self.A, self.G, self.Y
        PE, DV, AC, SW, ST, LD = self.PE, self.DV, self.AC, self.SW, self.ST, self.LD
        esr = ExitStack()
        R = lambda name, shape, dt: esr.enter_context(
            nc.sbuf_tensor(name, shape, dt, side="right"))
        SGW = R("SGW", [128, 4 * 1024], BF16)
        SDWB = R("SDWB", [128, 2 * 1024], BF16)
        SGTB = R("SGTB", [128, 8], BF16)
        SIG = R("SIG", [128, 16], F32)
        ACT_SH = R("ACT_SH", [128, 16 * 512], BF16)
        SSCR = R("SSCR", [128, 512], F32)
        SHST = R("SHST", [128, 1024], F32)

        G("dma_start", out=SGTB[:].rearrange("p (c n) -> p c n", n=1),
          in_=self.sgt_d.ap().rearrange("(c p) n -> p c n", p=128),
          allow_slow_non_contiguous=True, incs=[(self.SWR, 16)])
        sw_sg = self.SWR.n
        for t in range(NT):
            w = [(self.SWR, sw_sg), (DV, self.x1_done_dv)] if t == 0 else [(AC, AC.n)]
            for c in range(8):
                T("matmul", self.PS_A[:, 512:513],
                  self.X1T[:, c * 2048 + t * 128: c * 2048 + t * 128 + 128],
                  SGTB[:, c:c + 1],
                  start=(c == 0), stop=(c == 7),
                  waits=w if c == 0 else (), incs=[(PE, 1)])
            A("activation", SIG[:, t:t + 1], self.PS_A[:, 512:513], AF.Sigmoid,
              waits=[(PE, PE.n)], incs=[(AC, 1)])
        down_pe_prev = 0
        sh_st_prev = 0
        ld_marks = {}
        for tb in range(4):
            silu_ac = 0
            mult_dv = 0
            for fp in range(16):
                bofs = (fp % 2) * 2048
                wld = [(PE, ld_marks[fp % 2])] if (fp % 2) in ld_marks else []
                SWP = self.SWGA if fp % 2 == 0 else self.SWGB
                G("dma_start",
                  out=SGW[:, bofs:bofs + 1024].rearrange("p (c n) -> p c n", n=128),
                  in_=self.sgu_d.ap()[:, fp * 128:(fp + 1) * 128]
                      .rearrange("(c p) n -> p c n", p=128),
                  waits=wld, incs=[(SWP, 16)])
                G("dma_start",
                  out=SGW[:, bofs + 1024:bofs + 2048].rearrange("p (c n) -> p c n", n=128),
                  in_=self.sgu_d.ap()[:, (fp + 16) * 128:(fp + 17) * 128]
                      .rearrange("(c p) n -> p c n", p=128), incs=[(SWP, 16)])
                sw_f = SWP.n
                wmm = [(SWP, sw_f)]
                if silu_ac:
                    wmm.append((AC, silu_ac))
                if mult_dv:
                    wmm.append((DV, mult_dv))
                if tb > 0 and fp == 0:
                    wmm.append((PE, down_pe_prev))
                    wmm.append((AC, AC.n))
                    wmm.append((DV, DV.n))
                for half, pofs in ((0, 0), (1, 512)):
                    for c in range(8):
                        T("matmul", self.PS_BIG[:, pofs:pofs + 512],
                          SGW[:, bofs + half * 1024 + c * 128: bofs + half * 1024 + c * 128 + 128],
                          self.X1T[:, c * 2048 + tb * 512: c * 2048 + tb * 512 + 512],
                          start=(c == 0), stop=(c == 7),
                          waits=wmm if (half == 0 and c == 0) else (),
                          incs=[(PE, 1)] if (half == 1 and c == 7) else ())
                ld_marks[fp % 2] = PE.n
                A("activation", SSCR[:], self.PS_BIG[:, 0:512], AF.Silu,
                  waits=[(PE, PE.n)], incs=[(AC, 1)])
                silu_ac = AC.n
                Vx("tensor_tensor", ACT_SH[:, fp * 512:(fp + 1) * 512],
                   SSCR[:], self.PS_BIG[:, 512:1024], ALU.mult,
                   waits=[(AC, silu_ac)], incs=[(DV, 1)])
                mult_dv = DV.n
            act_dv = DV.n
            pj = [self.PS_BIG[:, 0:1024], self.PS_BIG[:, 1024:2048],
                  self.PS_A[:, 0:1024], None]
            for fp in range(16):
                bofs = (fp % 2) * 1024
                wld = []
                if ("d", fp % 2) in ld_marks:
                    wld.append((PE, ld_marks[("d", fp % 2)]))
                SWP2 = self.SWDA if fp % 2 == 0 else self.SWDB
                G("dma_start",
                  out=SDWB[:, bofs:bofs + 1024],
                  in_=self.sdw_d.ap()[fp * 128:(fp + 1) * 128, :],
                  waits=wld, incs=[(SWP2, 16)])
                sw_d = SWP2.n
                for j in range(3):
                    wdn = [(SWP2, sw_d), (DV, act_dv)]
                    if fp == 0:
                        wdn.append((AC, AC.n))
                    for nh in range(2):
                        T("matmul", pj[j][:, nh * 512:nh * 512 + 512],
                          ACT_SH[:, fp * 512 + j * 128: fp * 512 + j * 128 + 128],
                          SDWB[:, bofs + nh * 512: bofs + nh * 512 + 512],
                          start=(fp == 0), stop=(fp == 15),
                          waits=wdn if (j == 0 and nh == 0) else (),
                          incs=[(PE, 1)] if (j == 2 and nh == 1) else ())
                ld_marks[("d", fp % 2)] = PE.n
            for j in range(3):
                t = tb * 4 + j
                A("activation", SHST[:, 0:512], pj[j][:, 0:512], AF.Copy,
                  scale=SIG[:, t:t + 1], waits=[(PE, PE.n), (self.STB, sh_st_prev)], incs=[(AC, 1)])
                A("activation", SHST[:, 512:1024], pj[j][:, 512:1024], AF.Copy,
                  scale=SIG[:, t:t + 1], incs=[(AC, 1)])
                Y("dma_start", out=self.shf_dram.ap()[t * 128:(t + 1) * 128, :],
                  in_=SHST[:], waits=[(AC, AC.n)], incs=[(self.STB, 16)])
                sh_st_prev = self.STB.n
            for fp in range(16):
                bofs = (fp % 2) * 1024
                wld = [(PE, ld_marks[("d", fp % 2)])]
                SWP2 = self.SWDA if fp % 2 == 0 else self.SWDB
                G("dma_start",
                  out=SDWB[:, bofs:bofs + 1024],
                  in_=self.sdw_d.ap()[fp * 128:(fp + 1) * 128, :],
                  waits=wld, incs=[(SWP2, 16)])
                sw_d = SWP2.n
                wdn = [(SWP2, sw_d), (DV, act_dv)]
                if fp == 0:
                    wdn.append((AC, AC.n))
                for nh in range(2):
                    T("matmul", self.PS_A[:, nh * 512:nh * 512 + 512],
                      ACT_SH[:, fp * 512 + 3 * 128: fp * 512 + 3 * 128 + 128],
                      SDWB[:, bofs + nh * 512: bofs + nh * 512 + 512],
                      start=(fp == 0), stop=(fp == 15),
                      waits=wdn if nh == 0 else (),
                      incs=[(PE, 1)] if nh == 1 else ())
                ld_marks[("d", fp % 2)] = PE.n
            t = tb * 4 + 3
            A("activation", SHST[:, 0:512], self.PS_A[:, 0:512], AF.Copy,
              scale=SIG[:, t:t + 1], waits=[(PE, PE.n), (self.STB, sh_st_prev)], incs=[(AC, 1)])
            A("activation", SHST[:, 512:1024], self.PS_A[:, 512:1024], AF.Copy,
              scale=SIG[:, t:t + 1], incs=[(AC, 1)])
            Y("dma_start", out=self.shf_dram.ap()[t * 128:(t + 1) * 128, :],
              in_=SHST[:], waits=[(AC, AC.n)], incs=[(self.STB, 16)])
            sh_st_prev = self.STB.n
            down_pe_prev = PE.n
        self.shared_st = self.STB.n
        self.shared_ac = self.AC.n
        self.shared_dv = self.DV.n
        self.esrS = esr

    def phase_tables(self):
        nc, T, Vx, A, G, Y = self.nc, self.T, self.Vx, self.A, self.G, self.Y
        PE, DV, AC, SW, ST = self.PE, self.DV, self.AC, self.SW, self.ST
        esr = self.esrT = ExitStack()
        R = lambda name, shape, dt: esr.enter_context(
            nc.sbuf_tensor(name, shape, dt, side="right"))
        ATILE = R("ATILE", [64, 2 * 128], BF16)
        AMT = R("AMT", [8, 1024], F32)
        MT = R("MT", [8, 1024], F32)
        SCN = R("SCN", [8, 1024], F32)
        CARRY = R("CARRY", [8, 1], F32)
        MSCR = R("MSCR", [128, 8], F32)
        MKT = R("MKT", [64, 128], F32)
        self.SLM = R("SLM", [128, NT_ALL * 8], F32)
        self.AMN = R("AMN", [128, NT_ALL * 8], F32)
        CS = R("CS", [64, 128], F32)
        OCAR = R("OCAR", [128, 64], F32)
        CBS = R("CBS", [128, 64], F32)
        MK2 = R("MK2", [128, 64], F32)
        OH = R("OH", [128, 512], F32)
        NCH_SLM = (max(hi for _, hi in expert_windows()) + 7) // 8
        TAM = R("TAM", [128, NCH_SLM * 8 * 8 * 2], F32)
        for e in range(8):
            Y("dma_start",
              out=TAM[:, 0:NCH_SLM * 8 * 16]
                  .rearrange("p (t e x) -> p t (e x)", e=8, x=2)[:, :, e * 2:e * 2 + 1],
              in_=self.TIB[:, 0:NCH_SLM * 8].rearrange("p (t o) -> p t o", o=1),
              allow_slow_non_contiguous=True,
              waits=([(self.SWR, self.tib_swr)] if e == 0 else []),
              incs=[(self.LD, 16)])
        tam_ti_ld = self.LD.n

        Vx("memset", CARRY[:], 0.0, waits=[(self.CCS, self.cc2)], incs=[(DV, 1)])
        for ch in range(16):
            for tau in range(8):
                gt = ch * 8 + tau
                r, col = gt // 16, (gt % 16) * 128
                G("dma_start", out=ATILE[0:64, (tau % 2) * 128:(tau % 2) * 128 + 128],
                  in_=self.cc_at_out.ap()[r * 64:(r + 1) * 64, col:col + 128],
                  waits=[(PE, PE.n), (self.CCS, self.cc2)], incs=[(SW, 16)])
                if ch < NCH_SLM:
                    wsel = [(SW, SW.n), (DV, DV.n)]
                    if ch == 0 and tau == 0:
                        wsel += [(AC, self.shared_ac), (DV, self.shared_dv)]
                    T("matmul", self.PS_A[0:8, tau * 128:(tau + 1) * 128],
                      self.SEL8B[0:64, :], ATILE[0:64, (tau % 2) * 128:(tau % 2) * 128 + 128],
                      start=True, stop=True,
                      waits=wsel, incs=[(PE, 1)])
                Vx("tensor_single_scalar", MKT[:, :],
                   ATILE[0:64, (tau % 2) * 128:(tau % 2) * 128 + 128], 0.0,
                   ALU.not_equal, waits=[(SW, SW.n), (DV, DV.n)], incs=[(DV, 1)])
                Vx("reduce_sum", self.TSS[0:64, gt:gt + 1], MKT[:, :], axis=AX.X,
                   waits=[(DV, DV.n)], incs=[(DV, 1)])
            if ch >= NCH_SLM:
                continue
            Vx("tensor_copy", AMT[:], self.PS_A[0:8, 0:1024], waits=[(PE, PE.n)], incs=[(DV, 1)])
            Vx("tensor_single_scalar", MT[:], AMT[:], 0.0, ALU.not_equal,
               waits=[(DV, DV.n)], incs=[(DV, 1)])
            Vx("tensor_tensor_scan", SCN[:], MT[:], MT[:], CARRY[:, 0:1],
               ALU.add, ALU.max, waits=[(DV, DV.n)], incs=[(DV, 1)])
            Vx("tensor_copy", CARRY[:], SCN[:, 1023:1024], waits=[(DV, DV.n)], incs=[(DV, 1)])
            for tau in range(8):
                gt = ch * 8 + tau
                pofs = 1024 + (tau % 2) * 512
                T("transpose", self.PS_BIG[0:128, pofs:pofs + 8],
                  SCN[:, tau * 128:(tau + 1) * 128], self.IDENT[0:8, 0:8],
                  waits=[(DV, DV.n)], incs=[(PE, 1)])
                T("transpose", self.PS_BIG[0:128, pofs + 128:pofs + 136],
                  AMT[:, tau * 128:(tau + 1) * 128], self.IDENT[0:8, 0:8], incs=[(PE, 1)])
                Vx("tensor_copy", self.AMN[:, gt * 8:(gt + 1) * 8],
                   self.PS_BIG[:, pofs + 128:pofs + 136], waits=[(PE, PE.n)], incs=[(DV, 1)])
                Vx("tensor_copy",
                   TAM[:, gt * 16: gt * 16 + 16].rearrange("p (e x) -> p e x", x=2)[:, :, 1:2],
                   self.PS_BIG[:, pofs + 128:pofs + 136].rearrange("p (e o) -> p e o", o=1),
                   waits=[(DV, DV.n)], incs=[(DV, 1)])
                Vx("tensor_single_scalar", MSCR[:], self.AMN[:, gt * 8:(gt + 1) * 8], 0.0,
                   ALU.not_equal, waits=[(DV, DV.n)], incs=[(DV, 1)])
                Vx("scalar_tensor_tensor", self.SLM[:, gt * 8:(gt + 1) * 8],
                   MSCR[:], -(1.0 + 1048576.0), self.PS_BIG[:, pofs:pofs + 8],
                   ALU.mult, ALU.add, waits=[(DV, DV.n)], incs=[(DV, 1)])
                Vx("tensor_scalar", self.SLM[:, gt * 8:(gt + 1) * 8],
                   self.SLM[:, gt * 8:(gt + 1) * 8], 1048576.0, None, ALU.add,
                   waits=[(DV, DV.n)], incs=[(DV, 1)])
        Vx("tensor_tensor", CS[:, :], self.TSS[0:64, :],
           self.PRIORB[0:64, :], ALU.mult, waits=[(DV, DV.n)], incs=[(DV, 1)])
        Vx("reduce_sum", self.PROW[0:64, 0:1], CS[:, :], axis=AX.X,
           waits=[(DV, DV.n)], incs=[(DV, 1)])
        T("transpose", self.PS_BIG[0:1, 1024:1088], self.PROW[0:64, 0:1],
          self.IDENT[0:64, 0:64], waits=[(DV, DV.n), (PE, PE.n)], incs=[(PE, 1)])
        Vx("tensor_copy", OCAR[0:1, :], self.PS_BIG[0:1, 1024:1088],
           waits=[(PE, PE.n)], incs=[(DV, 1)])
        for j in range(NT):
            T("matmul", self.PS_A[:, 0:64], self.UB[:],
              self.M_OWN[:, j * 64:(j + 1) * 64],
              start=True, stop=True, waits=[(DV, DV.n)], incs=[(PE, 1)])
            T("matmul", self.PS_BIG[:, 1024:1088], self.ONEF[0:1, :],
              OCAR[0:1, :], start=True, stop=True, incs=[(PE, 1)])
            Vx("tensor_copy", CBS[:], self.PS_BIG[:, 1024:1088],
               waits=[(PE, PE.n)], incs=[(DV, 1)])
            Vx("tensor_tensor", self.SLOTS_OWN[:, j * 64:(j + 1) * 64],
               self.PS_A[:, 0:64], CBS[:], ALU.add, waits=[(DV, DV.n)], incs=[(DV, 1)])
            T("matmul", self.PS_A[0:1, 512:576], self.ONES[:, 0:1],
              self.M_OWN[:, j * 64:(j + 1) * 64],
              start=True, stop=True, waits=[(DV, DV.n)], incs=[(PE, 1)])
            Vx("tensor_tensor", OCAR[0:1, :], self.PS_A[0:1, 512:576], OCAR[0:1, :],
               ALU.add, waits=[(PE, PE.n), (DV, DV.n)], incs=[(DV, 1)])
        self.TBUF = R("TBUF", [128, EPC * 4], U32)
        self.ABUF = R("ABUF", [128, EPC * 4], F32)
        wins = expert_windows()
        evac_dv = {0: 0, 1: 0}
        oh_pe = {}
        tam_gated = False
        tcnt = 0
        self.tbl_dv_e = {}
        for e in range(EPC):
            for g in range(4):
                lo, hi = wins[g]
                bank = (e * 4 + g) % 2
                pofs = bank * 512
                first = True
                for tau in range(lo, hi):
                    par = tcnt % 4
                    tcnt += 1
                    OHp = OH[:, par * 128:(par + 1) * 128]
                    wv0 = [(DV, DV.n)]
                    if par in oh_pe:
                        wv0.append((PE, oh_pe[par]))
                    Vx("tensor_scalar", OHp, self.IOTA128B[:],
                       float(128 * g), self.SLM[:, tau * 8 + e: tau * 8 + e + 1],
                       ALU.add, ALU.is_equal, waits=wv0, incs=[(DV, 1)])
                    wmm = [(DV, DV.n)]
                    if first and evac_dv[bank]:
                        wmm.append((DV, evac_dv[bank]))
                    if not tam_gated:
                        wmm.append((self.LD, tam_ti_ld))
                        tam_gated = True
                    T("matmul", self.PS_A[:, pofs:pofs + 2], OHp,
                      TAM[:, (tau * 8 + e) * 2: (tau * 8 + e) * 2 + 2],
                      start=first, stop=(tau == hi - 1),
                      waits=wmm, incs=[(PE, 1)])
                    oh_pe[par] = PE.n
                    first = False
                Vx("tensor_copy", self.TBUF[:, e * 4 + g: e * 4 + g + 1],
                   self.PS_A[:, pofs:pofs + 1], waits=[(PE, PE.n)], incs=[(DV, 1)])
                Vx("tensor_copy", self.ABUF[:, e * 4 + g: e * 4 + g + 1],
                   self.PS_A[:, pofs + 1:pofs + 2], incs=[(DV, 1)])
                evac_dv[bank] = DV.n
                if g == 3:
                    self.tbl_dv_e[e] = DV.n
        FROWS = R("FROWS", [128, 64], F32)
        FF8 = R("FF8", [128, 8], F32)
        for j in range(NT):
            Vx("tensor_single_scalar", MK2[:], self.SLOTS_OWN[:, j * 64:(j + 1) * 64],
               float(CAP - 1), ALU.is_gt, waits=[(DV, DV.n)], incs=[(DV, 1)])
            Vx("scalar_tensor_tensor", FROWS[:], MK2[:], 2097152.0,
               self.SLOTS_OWN[:, j * 64:(j + 1) * 64], ALU.mult, ALU.add,
               waits=[(DV, DV.n)], incs=[(DV, 1)])
            Vx("tensor_add", FROWS[:], FROWS[:], self.ROWB[:], waits=[(DV, DV.n)], incs=[(DV, 1)])
            for kk in range(8):
                Vx("tensor_scalar", MK2[:], self.IOTA64B[:],
                   self.IDXF[:, j * 8 + kk: j * 8 + kk + 1], None, ALU.is_equal,
                   waits=[(DV, DV.n)], incs=[(DV, 1)])
                Vx("tensor_tensor", MK2[:], FROWS[:], MK2[:], ALU.mult,
                   waits=[(DV, DV.n)], incs=[(DV, 1)])
                Vx("reduce_sum", FF8[:, kk:kk + 1], MK2[:], axis=AX.X,
                   waits=[(DV, DV.n)], incs=[(DV, 1)])
            Vx("tensor_copy", self.FU[:, j * 8:(j + 1) * 8], FF8[:],
               waits=[(DV, DV.n)], incs=[(DV, 1)])
            Vx("tensor_single_scalar", self.VALID[:, j * 8:(j + 1) * 8],
               FF8[:], float(NCORE * EPC * CAP - 1), ALU.is_le, incs=[(DV, 1)])
        self.tables_dv = DV.n
        self.tables_pe_end = PE.n
        tp = self.tap("slm", [128, NT_ALL * 8])
        if tp is not None:
            self.Y("dma_start", out=tp.ap(), in_=self.SLM[:], waits=[(DV, DV.n)], incs=[(ST, 16)])
        tp = self.tap("amn", [128, NT_ALL * 8])
        if tp is not None:
            self.Y("dma_start", out=tp.ap(), in_=self.AMN[:], waits=[(DV, DV.n)], incs=[(ST, 16)])
        tp = self.tap("sown", [128, NT * 64])
        if tp is not None:
            self.Y("dma_start", out=tp.ap(), in_=self.SLOTS_OWN[:], waits=[(DV, DV.n)], incs=[(ST, 16)])
        tp = self.tap("tbuf", [128, EPC * 4], U32)
        if tp is not None:
            self.Y("dma_start", out=tp.ap(), in_=self.TBUF[:], waits=[(DV, DV.n)], incs=[(ST, 16)])
        tp = self.tap("abuf", [128, EPC * 4])
        if tp is not None:
            self.Y("dma_start", out=tp.ap(), in_=self.ABUF[:], waits=[(DV, DV.n)], incs=[(ST, 16)])
        tp = self.tap("fu", [128, NT * 8], U32)
        if tp is not None:
            self.Y("dma_start", out=tp.ap(), in_=self.FU[:], waits=[(DV, DV.n)], incs=[(ST, 16)])

    def phase_experts(self):
        nc, T, Vx, A, G, Y = self.nc, self.T, self.Vx, self.A, self.G, self.Y
        PE, DV, AC, SW, ST = self.PE, self.DV, self.AC, self.SW, self.ST
        esr = self.esrE = ExitStack()
        R = lambda name, shape, dt: esr.enter_context(
            nc.sbuf_tensor(name, shape, dt, side="right"))
        BUFR = R("BUFR", [128, 2 * 1024], BF16)
        BUFT = R("BUFT", [128, 8 * 512], BF16)
        ACTE = R("ACTE", [128, 4 * 512], BF16)
        SSCR2 = R("SSCR2", [128, 512], F32)
        OSTG = R("OSTG", [128, 1024], F32)

        self.bc_x1 = self.nc.gpsimd.alloc_register("bc_x1")
        self.bc_ob = self.nc.gpsimd.alloc_register("bc_ob")
        G("reg_mov", self.bc_x1, T_ALL - 1)
        G("reg_mov", self.bc_ob, NCORE * EPC * CAP - 1)
        gath_sw = {}
        ostg_st = 0
        self.cc3a = None
        CHUNK_AG3 = True
        for e in range(EPC):
            if e == 4 and CHUNK_AG3:
                self.G("collective_compute", "AllGather", ALU.bypass,
                       replica_groups=[list(range(NCORE))],
                       ins=[self.cc_ob_in.ap()[0:2048, :]],
                       outs=[self.cc_ob_out.ap()[0:16384, :]],
                       waits=[(self.SWE, self.SWE.n)], incs=[(self.CCS, 1)])
                self.cc3a = self.CCS.n
            wld = [(PE, PE.n)] if e > 0 else [(PE, self.proj_done_pe)]
            G("dma_start", out=self.W1B[:].rearrange("p (c n) -> p c n", n=1024),
              in_=self.egu_d.ap()[e].rearrange("(c p) n -> p c n", p=128),
              waits=wld, incs=[(self.SWE, 16)])
            G("dma_start", out=self.W2B[:].rearrange("p (c n) -> p c n", n=1024),
              in_=self.edw_d.ap()[e].rearrange("(c p) n -> p c n", p=128),
              incs=[(self.SWE, 16)])
            w_sw = self.SWE.n
            for st4 in range(4):
                SWF = self.SWFA if st4 % 2 == 0 else self.SWFB
                G("indirect_dma_start",
                  out=BUFR[:, (st4 % 2) * 1024:(st4 % 2) * 1024 + 1024],
                  out_offset=None,
                  in_=self.cc_x1_out.ap(),
                  in_offset=IndirectOffsetOnAxis(self.TBUF[:, e * 4 + st4: e * 4 + st4 + 1], 0),
                  bounds_check=self.bc_x1, oob_is_err=False,
                  waits=[(DV, self.tbl_dv_e[e]), (self.CCS, self.cc1),
                         (PE, gath_sw.get(st4 % 2, 0))],
                  incs=[(SWF, 16)])
                sw_g = SWF.n
                for c in range(8):
                    jj = st4 * 8 + c
                    pofs = (jj % 2) * 1024
                    T("transpose", self.PS_BF[:, pofs:pofs + 128],
                      BUFR[:, (st4 % 2) * 1024 + c * 128: (st4 % 2) * 1024 + c * 128 + 128],
                      self.IDENTBF[:],
                      waits=[(SWF, sw_g), (DV, DV.n)], incs=[(PE, 1)])
                    Vx("tensor_copy", self.BUFT_slice(BUFT, c, st4),
                       self.PS_BF[:, pofs:pofs + 128], waits=[(PE, PE.n)], incs=[(DV, 1)])
                gath_sw[st4 % 2] = PE.n
            buft_dv = DV.n
            silu_ac = 0
            mult_dv = 0
            for f in range(4):
                wmm = [(self.SWE, w_sw), (DV, buft_dv)]
                if silu_ac:
                    wmm += [(AC, silu_ac), (DV, mult_dv)]
                for half, fch in ((0, f), (1, f + 4)):
                    for c in range(8):
                        T("matmul", self.PS_BIG[:, half * 512:half * 512 + 512],
                          self.W1B[:, c * 1024 + fch * 128: c * 1024 + fch * 128 + 128],
                          BUFT[:, c * 512:(c + 1) * 512],
                          start=(c == 0), stop=(c == 7),
                          waits=wmm if (half == 0 and c == 0) else (), incs=[(PE, 1)])
                A("activation", SSCR2[:], self.PS_BIG[:, 0:512], AF.Silu,
                  waits=[(PE, PE.n)], incs=[(AC, 1)])
                silu_ac = AC.n
                Vx("tensor_tensor", ACTE[:, f * 512:(f + 1) * 512], SSCR2[:],
                   self.PS_BIG[:, 512:1024], ALU.mult, waits=[(AC, silu_ac)], incs=[(DV, 1)])
                mult_dv = DV.n
            for st4 in range(4):
                wdn = [(DV, mult_dv)]
                if True:
                    wdn.append((AC, AC.n))
                for c2 in range(4):
                    for nh in range(2):
                        T("matmul", self.PS_A[:, nh * 512:nh * 512 + 512],
                          ACTE[:, c2 * 512 + st4 * 128: c2 * 512 + st4 * 128 + 128],
                          self.W2B[:, c2 * 1024 + nh * 512: c2 * 1024 + nh * 512 + 512],
                          start=(c2 == 0), stop=(c2 == 3),
                          waits=wdn if (c2 == 0 and nh == 0) else (), incs=[(PE, 1)])
                A("activation", OSTG[:, 0:512], self.PS_A[:, 0:512], AF.Copy,
                  scale=self.ABUF[:, e * 4 + st4: e * 4 + st4 + 1],
                  waits=[(PE, PE.n), (self.SWE, ostg_st)], incs=[(AC, 1)])
                A("activation", OSTG[:, 512:1024], self.PS_A[:, 512:1024], AF.Copy,
                  scale=self.ABUF[:, e * 4 + st4: e * 4 + st4 + 1], incs=[(AC, 1)])
                G("dma_start", out=self.cc_ob_in.ap()[e * CAP + st4 * 128: e * CAP + st4 * 128 + 128, :],
                  in_=OSTG[:], waits=[(AC, AC.n)], incs=[(self.SWE, 16)])
                ostg_st = self.SWE.n
        tp = self.tap("obuf", [EPC * CAP, D], BF16)
        if tp is not None:
            G("dma_start", out=tp.ap(), in_=self.cc_ob_in.ap(),
              waits=[(self.SWE, self.SWE.n)], incs=[(self.SWE, 16)])
        w3b = [(self.SWE, self.SWE.n)]
        if self.cc3a is not None:
            w3b.append((self.CCS, self.cc3a))
            self.G("collective_compute", "AllGather", ALU.bypass,
                   replica_groups=[list(range(NCORE))],
                   ins=[self.cc_ob_in.ap()[2048:4096, :]],
                   outs=[self.cc_ob_out.ap()[16384:32768, :]],
                   waits=w3b, incs=[(self.CCS, 1)])
        else:
            self.G("collective_compute", "AllGather", ALU.bypass,
                   replica_groups=[list(range(NCORE))],
                   ins=[self.cc_ob_in.ap()],
                   outs=[self.cc_ob_out.ap()],
                   waits=w3b, incs=[(self.CCS, 1)])
        self.cc3 = self.CCS.n
        tp2 = self.tap("oball", [NCORE * EPC * CAP, D], BF16)
        if tp2 is not None:
            G("dma_start", out=tp2.ap(), in_=self.cc_ob_out.ap(),
              waits=[(self.CCS, self.cc3)], incs=[(self.SWE, 16)])

    @staticmethod
    def BUFT_slice(BUFT, c, st4):
        return BUFT[:, c * 512 + st4 * 128: c * 512 + st4 * 128 + 128]

    def phase_combine(self):
        nc, T, Vx, A, G, Y = self.nc, self.T, self.Vx, self.A, self.G, self.Y
        PE, DV, AC, SW, ST, LD, GP = self.PE, self.DV, self.AC, self.SW, self.ST, self.LD, self.GP
        self.esrE.close()
        self.esrT.close()
        self.esrS.close()
        self.esrR.close()
        esr = ExitStack()
        R = lambda name, shape, dt: esr.enter_context(
            nc.sbuf_tensor(name, shape, dt, side="right"))
        GT = R("GT", [128, 2 * 8 * 1024], BF16)
        YS2 = R("YS2", [128, 2 * 1024], F32)
        X1L = R("X1L", [128, 2 * 1024], F32)
        SHL = R("SHL", [128, 2 * 1024], F32)
        MV2 = R("MV2", [128, 16], F32)

        Vx("memset", GT[:], 0.0,
           waits=[(self.CCS, self.cc3), (PE, self.tables_pe_end),
                  (DV, self.tables_dv)], incs=[(DV, 1)])
        gt_init_dv = DV.n
        gt_cons = {}
        x1_cons = {}
        sh_cons = {}
        y2_cons = {}
        tp = self.tap("ycomb", [S, D])
        for j in range(NT):
            half = j % 2
            gts = GT[:, half * 8192:(half + 1) * 8192]
            y2 = YS2[:, half * 1024:half * 1024 + 1024]
            x1l = X1L[:, half * 1024:half * 1024 + 1024]
            shl = SHL[:, half * 1024:half * 1024 + 1024]
            SWH = self.SWHA if half == 0 else self.SWHB
            LDP = self.LDA if half == 0 else self.LDB
            wms = [(self.CCS, self.cc3), (DV, gt_cons.get(half, gt_init_dv))]
            if j < 2:
                wms.append((self.STB, self.shared_st))
            for kk in range(8):
                G("indirect_dma_start", out=gts[:, kk * 1024:(kk + 1) * 1024],
                  out_offset=None,
                  in_=self.cc_ob_out.ap(),
                  in_offset=IndirectOffsetOnAxis(self.FU[:, j * 8 + kk: j * 8 + kk + 1], 0),
                  bounds_check=self.bc_ob, oob_is_err=False,
                  waits=wms if kk == 0 else (), incs=[(SWH, 16)])
            sw_g = SWH.n
            wx1 = [(self.STB, self.shared_st), (DV, x1_cons.get(half, 0))]
            wsh = [(self.STB, self.shared_st), (DV, sh_cons.get(half, 0))]
            if j < 2:
                wx1 += [(self.CCS, self.cc3), (PE, self.tables_pe_end),
                        (DV, self.tables_dv)]
                wsh += [(self.CCS, self.cc3), (PE, self.tables_pe_end),
                        (DV, self.tables_dv)]
            Y("dma_start", out=x1l, in_=self.x1f_dram.ap()[j * 128:(j + 1) * 128, :],
              waits=wx1, incs=[(LDP, 16)])
            Y("dma_start", out=shl, in_=self.shf_dram.ap()[j * 128:(j + 1) * 128, :],
              waits=wsh, incs=[(LDP, 16)])
            ld_j = LDP.n
            Vx("tensor_scalar_mul", y2, gts[:, 0:1024],
               self.VALID[:, j * 8: j * 8 + 1],
               waits=[(SWH, sw_g), (ST, y2_cons.get(half, 0)), (DV, DV.n)],
               incs=[(DV, 1)])
            for kk in range(1, 8):
                Vx("scalar_tensor_tensor", y2, gts[:, kk * 1024:(kk + 1) * 1024],
                   self.VALID[:, j * 8 + kk: j * 8 + kk + 1], y2,
                   ALU.mult, ALU.add, waits=[(DV, DV.n)], incs=[(DV, 1)])
            if tp is not None:
                Y("dma_start", out=tp.ap()[j * 128:(j + 1) * 128, :], in_=y2,
                  waits=[(DV, DV.n)], incs=[(ST, 16)])
                self.q["vector"].append(
                    lambda En, st=ST.n: En.wait_ge(ST.h, st))
            Vx("tensor_add", y2, y2, x1l, waits=[(LDP, ld_j), (DV, DV.n)], incs=[(DV, 1)])
            gt_cons[half] = DV.n
            x1_cons[half] = DV.n
            Vx("tensor_add", y2, y2, shl, waits=[(DV, DV.n)], incs=[(DV, 1)])
            sh_cons[half] = DV.n
            tpt = self.tap("ytot", [S, D])
            if tpt is not None:
                Y("dma_start", out=tpt.ap()[j * 128:(j + 1) * 128, :], in_=y2,
                  waits=[(DV, DV.n)], incs=[(ST, 16)])
                self.q["vector"].append(
                    lambda En, st=ST.n: En.wait_ge(ST.h, st))
            Vx("bn_stats", MV2[:, 0:6], y2[:, 0:512], waits=[(DV, DV.n)], incs=[(DV, 1)])
            Vx("bn_stats", MV2[:, 6:12], y2[:, 512:1024], incs=[(DV, 1)])
            Vx("bn_aggr", MV2[:, 12:14], MV2[:, 0:12], waits=[(DV, DV.n)], incs=[(DV, 1)])
            Vx("tensor_scalar", MV2[:, 14:15], MV2[:, 13:14], 1e-5, None, ALU.add,
               waits=[(DV, DV.n)], incs=[(DV, 1)])
            A("sqrt", MV2[:, 15:16], MV2[:, 14:15], waits=[(DV, DV.n)], incs=[(AC, 1)])
            Vx("reciprocal", MV2[:, 10:11], MV2[:, 15:16], waits=[(AC, AC.n)], incs=[(DV, 1)])
            Vx("tensor_scalar", y2, y2, MV2[:, 12:13], MV2[:, 10:11],
               ALU.subtract, ALU.mult, waits=[(DV, DV.n)], incs=[(DV, 1)])
            Y("dma_start", out=self.out_d.ap()[j * 128:(j + 1) * 128, :], in_=y2,
              waits=[(DV, DV.n)], incs=[(ST, 16)])
            y2_cons[half] = ST.n
        Y("nop", waits=[(ST, ST.n), (SW, SW.n), (self.SWHA, self.SWHA.n),
                        (self.SWHB, self.SWHB.n), (self.LDA, self.LDA.n),
                        (self.LDB, self.LDB.n), (self.STB, self.STB.n)])


def make_inputs_per_core(ins):
    x = np.asarray(ins["L0_attn_x"], np.float32)
    wq = np.asarray(ins["L0_attn_wq"], np.float32)
    wkv = np.asarray(ins["L0_attn_w_kv_down"], np.float32)
    wku = np.asarray(ins["L0_attn_w_k_up"], np.float32)
    wvu = np.asarray(ins["L0_attn_w_v_up"], np.float32)
    wo = np.asarray(ins["L0_attn_wo"], np.float32)
    base = {
        "wqT": np.ascontiguousarray(wq.T),
        "wkvT": np.ascontiguousarray(wkv.T),
        "wkuT": np.ascontiguousarray(wku.T),
        "wvuT": np.ascontiguousarray(wvu.T),
        "woT": np.ascontiguousarray(wo.T),
        "sgu": np.ascontiguousarray(ins["L0_ffn_shared_gate_up_w"], ).astype(np.float32, copy=False),
        "sdw": np.ascontiguousarray(ins["L0_ffn_shared_down_w"]).astype(np.float32, copy=False),
        "sgt": np.ascontiguousarray(ins["L0_ffn_shared_gate_w"]).astype(np.float32, copy=False),
        "gw": np.ascontiguousarray(ins["L0_ffn_gate_w"]).astype(np.float32, copy=False),
    }
    egu = np.asarray(ins["L0_ffn_expert_gate_up_w"], np.float32)
    edw = np.asarray(ins["L0_ffn_expert_down_w"], np.float32)
    maps = []
    for c in range(NCORE):
        m = dict(base)
        m["x"] = np.ascontiguousarray(x[c])
        m["xT"] = np.ascontiguousarray(x[c].T)
        myexp = list(range(4 * c, 4 * c + 4)) + list(range(32 + 4 * c, 32 + 4 * c + 4))
        m["egu"] = np.ascontiguousarray(egu[myexp])
        m["edw"] = np.ascontiguousarray(edw[myexp])
        sel = np.zeros((E, EPC), np.float32)
        for j, e in enumerate(myexp):
            sel[e, j] = 1.0
        m["sel8"] = sel
        pri = np.zeros((E, 128), np.float32)
        pri[:, : c * NT] = 1.0
        m["prior"] = pri
        maps.append(m)
    return maps


def build_nc(debug_taps=(), phases=99):
    b = Builder(debug_taps=debug_taps, phases=phases)
    b.phase_a0()
    b.phase_a1()
    if phases >= 2:
        b.phase_a2()
    if phases >= 3:
        b.phase_attn()
    if phases >= 4 and phases < 5:
        b.phase_wo_ln()
        b.finish_stub()
    elif phases >= 5:
        b.phase_wo_ln()
        b.phase_coll1()
        b.phase_router()
        b.phase_shared()
        b.phase_tables()
        b.phase_experts()
        b.phase_combine()
    else:
        b.finish_debug(phases)
    b.emit_all()
    return b



def kernel(**inputs):
    b = build_nc(debug_taps=(), phases=9)
    maps = make_inputs_per_core(inputs)
    res = run_bass_kernel_spmd(b.nc, maps, core_ids=list(range(NCORE)))
    return np.stack([np.asarray(res.results[c]["out"], np.float32)
                     for c in range(NCORE)], axis=0)


# version 21 (final)
# speedup vs baseline: 1.0200x; 1.0200x over previous
import os
import numpy as np
import ml_dtypes
import concourse.bass as bass
import concourse.mybir as mybir
from concourse.ap import AP
from concourse.bass_utils import run_bass_kernel_spmd
from concourse.bass_core_rust import IndirectOffsetOnAxis
from contextlib import ExitStack

F32 = mybir.dt.float32
BF16 = mybir.dt.bfloat16
U32 = mybir.dt.uint32
AF = mybir.ActivationFunctionType
ALU = mybir.AluOpType
AX = mybir.AxisListType

NCORE = 8
S, D, NH, HD, LAT = 2048, 1024, 16, 64, 512
NT = S // 128
E, EPC, TOPK, CAP = 64, 8, 8, 512
ISH, IEX = 2048, 512
T_ALL = NCORE * S
NT_ALL = T_ALL // 128
SHIFT = 20.0
bfdt = ml_dtypes.bfloat16
ENGS = ("tensor", "vector", "scalar", "gpsimd", "sync")


def host_consts():
    c = {}
    c["ident_f32"] = np.eye(128, dtype=np.float32)
    c["ident_bf"] = np.eye(128, dtype=bfdt)
    inv_freq = (1.0 / (10000.0 ** (np.arange(0, HD, 2, dtype=np.float32) / HD))).astype(np.float32)
    pos = np.arange(S, dtype=np.float32)
    freqs = pos[:, None] * inv_freq[None, :]
    emb = np.concatenate([freqs, freqs], axis=1)
    cos = np.cos(emb).astype(np.float32)
    sin = np.sin(emb).astype(np.float32)
    sinS = sin.copy()
    sinS[:, :32] *= -1.0
    c["cosQ"] = (cos * 0.125).astype(bfdt)
    c["sinQ"] = (sinS * 0.125).astype(bfdt)
    c["cosK"] = cos.astype(bfdt)
    c["sinK"] = sinS.astype(bfdt)
    p = np.arange(128)[:, None]
    f = np.arange(512)[None, :]
    c["mask01"] = np.ascontiguousarray(
        np.stack([((128 * j + p) <= f) for j in range(4)]).astype(bfdt))
    c["Ustrict"] = (np.arange(128)[:, None] < np.arange(128)[None, :]).astype(bfdt)
    c["iota128"] = np.broadcast_to(np.arange(128, dtype=np.float32), (128, 128)).copy()
    c["iota64"] = np.broadcast_to(np.arange(64, dtype=np.float32), (128, 64)).copy()
    c["TI"] = np.ascontiguousarray(
        np.arange(128, dtype=np.float32)[:, None]
        + 128.0 * np.arange(NT_ALL, dtype=np.float32)[None, :])
    rb = np.zeros(E, np.float32)
    for e in range(E):
        if e < 32:
            rb[e] = (e // 4) * 2048 + (e % 4) * 512
        else:
            rb[e] = 16384 + ((e - 32) // 4) * 2048 + ((e - 32) % 4) * 512
    c["rowbase"] = np.broadcast_to(rb, (128, E)).copy()
    sh = np.zeros((128, 128), dtype=bfdt)
    sh[np.arange(64), np.arange(64) + 64] = 1.0
    c["shift64"] = sh
    return c


def expert_windows(margin_sig=8.0):
    wins = []
    pr = 1.0 / 8.0
    ts = np.arange(T_ALL + 1, dtype=np.float64)
    mean, sig = ts * pr, np.sqrt(ts * pr * (1 - pr))
    for g in range(4):
        if g == 0:
            lo = 0
        else:
            ok = np.nonzero(mean + margin_sig * sig >= 128 * g)[0]
            lo = int(ok[0]) // 128 if len(ok) else 0
        ok = np.nonzero(mean - margin_sig * sig <= 128 * (g + 1))[0]
        hi = min(NT_ALL, int(ok[-1]) // 128 + 1) if len(ok) else NT_ALL
        wins.append((lo, hi))
    return wins


def bc(ap, n, pos=1):
    dims = list(ap.ap)
    dims.insert(pos, [0, n])
    return AP(ap.tensor, ap.offset, dims)


class Sem:
    def __init__(self, es, nc, name):
        self.h = es.enter_context(nc.semaphore(name))
        self.n = 0
        self.name = name


class Builder:
    def __init__(self, debug_taps=(), phases=99):
        self.phases = phases
        nc = self.nc = bass.Bass("TRN2", target_bir_lowering=False)
        self.Ch = host_consts()
        self.q = {e: [] for e in ENGS}
        self.taps = {}
        self.debug_taps = debug_taps

        din = lambda name, shape, dt=F32: nc.dram_tensor(name, shape, dt, kind="ExternalInput")
        self.xT_d = din("xT", [D, S])
        self.x_d = din("x", [S, D])
        self.wqT_d = din("wqT", [D, D])
        self.wkvT_d = din("wkvT", [D, LAT])
        self.wkuT_d = din("wkuT", [LAT, D])
        self.wvuT_d = din("wvuT", [LAT, D])
        self.woT_d = din("woT", [D, D])
        self.sgu_d = din("sgu", [D, 2 * ISH])
        self.sdw_d = din("sdw", [ISH, D])
        self.sgt_d = din("sgt", [D, 1])
        self.gw_d = din("gw", [D, E])
        self.egu_d = din("egu", [EPC, D, 2 * IEX])
        self.edw_d = din("edw", [EPC, IEX, D])
        self.sel8_d = din("sel8", [E, EPC])
        self.prior_d = din("prior", [E, 128])

        self.out_d = nc.dram_tensor("out", [S, D], F32, kind="ExternalOutput")

        self.cc_x1_in = nc.dram_tensor("cc_x1_in", [S, D], BF16)
        self.cc_x1_out = nc.dram_tensor("cc_x1_out", [T_ALL, D], BF16)
        self.cc_at_in = nc.dram_tensor("cc_at_in", [E, S], F32)
        self.cc_at_out = nc.dram_tensor("cc_at_out", [NCORE * E, S], F32)
        self.cc_ob_in = nc.dram_tensor("cc_ob_in", [EPC * CAP, D], BF16)
        self.cc_ob_out = nc.dram_tensor("cc_ob_out", [NCORE * EPC * CAP, D], BF16)
        self.x1f_dram = nc.dram_tensor("x1f_dram", [S, D], F32)
        self.shf_dram = nc.dram_tensor("shf_dram", [S, D], F32)

        ic = nc.inline_tensor
        self.ident_c = ic(self.Ch["ident_f32"], name="ident_c")
        self.identbf_c = ic(self.Ch["ident_bf"], name="identbf_c")
        self.cosQ_c = ic(self.Ch["cosQ"], name="cosQ_c")
        self.sinQ_c = ic(self.Ch["sinQ"], name="sinQ_c")
        self.cosK_c = ic(self.Ch["cosK"], name="cosK_c")
        self.sinK_c = ic(self.Ch["sinK"], name="sinK_c")
        self.mask01_c = ic(self.Ch["mask01"], name="mask01_c")
        self.Ustrict_c = ic(self.Ch["Ustrict"], name="Ustrict_c")
        self.iota128_c = ic(self.Ch["iota128"], name="iota128_c")
        self.iota64_c = ic(self.Ch["iota64"], name="iota64_c")
        self.TI_c = ic(self.Ch["TI"], name="TI_c")
        self.shift64_c = ic(self.Ch["shift64"], name="shift64_c")
        self.rowbase_c = ic(self.Ch["rowbase"], name="rowbase_c")

        esg = self.esg = ExitStack()
        self.PS_BIG = esg.enter_context(nc.psum_tensor("PS_BIG", [128, 2048], F32))
        self.PS_A = esg.enter_context(nc.psum_tensor("PS_A", [128, 1024], F32))
        self.PS_BF = esg.enter_context(nc.psum_tensor("PS_BF", [128, 2048], BF16))
        for s in ("LD", "SW", "PE", "DV", "AC", "GP", "CCS", "ST",
                  "SWR", "SWGA", "SWGB", "SWDA", "SWDB", "SWE",
                  "SWFA", "SWFB", "SWHA", "SWHB", "LDA", "LDB", "STA", "STB"):
            setattr(self, s, Sem(esg, nc, s))

        esl = self.esl = ExitStack()
        L = lambda name, shape, dt=F32: esl.enter_context(
            nc.sbuf_tensor(name, shape, dt, side="left"))
        self.IDENT = L("IDENT", [128, 128], F32)
        self.IDENTBF = L("IDENTBF", [128, 128], BF16)
        self.SHIFT64 = L("SHIFT64", [128, 128], BF16)
        self.ONES = L("ONES", [128, 128], BF16)
        self.ONEF = L("ONEF", [128, 128], F32)
        self.MASK01 = L("MASK01", [128, 4 * 512], BF16)
        self.W1B = L("W1B", [128, 8 * 1024], BF16)
        self.W2B = L("W2B", [128, 4 * 1024], BF16)
        self.RS1 = L("RS1", [128, 1024], F32)
        self.RS2 = L("RS2", [128, 1024], F32)
        self.RS3 = L("RS3", [128, 1024], F32)
        self.ZROW = L("ZROW", [128, 1024], F32)
        self.NEGSH = L("NEGSH", [128, 1], F32)
        esqkv = self.esqkv = ExitStack()
        Lq = lambda name, shape, dt: esqkv.enter_context(
            nc.sbuf_tensor(name, shape, dt, side="left"))
        self.COSQ = Lq("COSQ", [128, NT * HD], BF16)
        self.SINQ = Lq("SINQ", [128, NT * HD], BF16)
        self.COSK = Lq("COSK", [128, NT * HD], BF16)
        self.SINK = Lq("SINK", [128, NT * HD], BF16)
        self.QR = Lq("QR", [128, NT * 1024], BF16)
        self.KR = Lq("KR", [128, NT * 1024], BF16)
        self.V = Lq("V", [128, NT * NH * 65], BF16)

        esr = self.esr = ExitStack()
        self.XT = esr.enter_context(nc.sbuf_tensor("XT", [128, 8 * 2048], BF16, side="right"))
        self.KVT = esr.enter_context(nc.sbuf_tensor("KVT", [128, 4 * 2048], BF16, side="right"))

    def tap(self, name, shape, dt=F32):
        if name in self.debug_taps and name not in self.taps:
            self.taps[name] = self.nc.dram_tensor("tap_" + name, shape, dt, kind="ExternalOutput")
        return self.taps.get(name)

    def op(self, eng, meth, *args, waits=(), incs=(), **kwargs):
        waits = tuple(waits)
        inc2 = []
        for s, kk in incs:
            s.n += kk
            inc2.append((s, kk))

        def emit(E, meth=meth, args=args, kwargs=kwargs, waits=waits, incs=tuple(inc2)):
            for s, v in waits:
                E.wait_ge(s.h, v)
            inst = getattr(E, meth)(*args, **kwargs)
            for s, kk in incs:
                inst.then_inc(s.h, kk)
            return inst
        self.q[eng].append(emit)

    def T(self, *a, **kw): self.op("tensor", *a, **kw)
    def Vx(self, *a, **kw): self.op("vector", *a, **kw)
    def A(self, *a, **kw): self.op("scalar", *a, **kw)
    def G(self, *a, **kw): self.op("gpsimd", *a, **kw)
    def Y(self, *a, **kw): self.op("sync", *a, **kw)

    def emit_all(self):
        with self.nc.Block("main") as blk:
            for eng in ENGS:
                fns = self.q[eng]
                if fns:
                    getattr(blk, eng)(lambda En, fns=fns: [f(En) for f in fns])

    @staticmethod
    def tl(buf, i, w):
        return buf[:, i * w:(i + 1) * w]

    def rope(self, t, psum, cosb, sinb, dstbuf):
        AC, DV, PE = self.AC, self.DV, self.PE
        self.A("copy", self.RS1[:], psum, waits=[(PE, PE.n), (DV, DV.n)], incs=[(AC, 1)])
        ac_evac = AC.n
        r1 = self.RS1[:].rearrange("p (h half d) -> p h half d", h=NH, half=2, d=32)
        r2 = self.RS2[:].rearrange("p (h half d) -> p h half d", h=NH, half=2, d=32)
        sv = self.tl(sinb, t, HD).rearrange("p (half d) -> p half d", half=2)
        self.Vx("tensor_tensor", r2[:, :, 0, :], r1[:, :, 1, :], bc(sv[:, 0, :], NH),
                ALU.mult, waits=[(AC, AC.n)], incs=[(DV, 1)])
        self.Vx("tensor_tensor", r2[:, :, 1, :], r1[:, :, 0, :], bc(sv[:, 1, :], NH),
                ALU.mult, incs=[(DV, 1)])
        self.Vx("tensor_tensor", self.RS3[:].rearrange("p (h d) -> p h d", d=HD),
                self.RS1[:].rearrange("p (h d) -> p h d", d=HD),
                bc(self.tl(cosb, t, HD), NH), ALU.mult, incs=[(DV, 1)])
        self.Vx("tensor_add", self.tl(dstbuf, t, 1024), self.RS3[:], self.RS2[:],
                waits=[(DV, DV.n)], incs=[(DV, 1)])
        return ac_evac, DV.n

    def phase_a0(self):
        G, Y, LD, SW, GP = self.G, self.Y, self.LD, self.SW, self.GP
        Y("dma_start", out=self.IDENT[:], in_=self.ident_c.ap(), incs=[(LD, 16)])
        for dst, src in ((self.COSQ, self.cosQ_c), (self.SINQ, self.sinQ_c),
                         (self.COSK, self.cosK_c), (self.SINK, self.sinK_c)):
            G("dma_start", out=dst[:].rearrange("p (t d) -> p t d", d=HD),
              in_=src.ap().rearrange("(t p) d -> p t d", p=128), incs=[(SW, 16)])
        G("dma_start", out=self.IDENTBF[:], in_=self.identbf_c.ap(), incs=[(SW, 16)])
        G("dma_start", out=self.SHIFT64[:], in_=self.shift64_c.ap(), incs=[(SW, 16)])
        G("dma_start", out=self.MASK01[:].rearrange("p (j f) -> p j f", f=512),
          in_=self.mask01_c.ap().rearrange("j p f -> p j f"), incs=[(SW, 16)])
        G("dma_start", out=self.XT[:].rearrange("p (c t) -> p c t", t=S),
          in_=self.xT_d.ap().rearrange("(c p) t -> p c t", p=128), incs=[(SW, 16)])
        G("dma_start", out=self.W1B[:].rearrange("p (c n) -> p c n", n=1024),
          in_=self.wqT_d.ap().rearrange("(c p) n -> p c n", p=128), incs=[(SW, 16)])
        G("dma_start", out=self.W2B[:].rearrange("p (c n) -> p c n", n=LAT),
          in_=self.wkvT_d.ap().rearrange("(c p) n -> p c n", p=128), incs=[(SW, 16)])
        G("memset", self.ONES[:], 1.0, incs=[(GP, 1)])
        G("memset", self.NEGSH[:], -SHIFT, incs=[(GP, 1)])
        self.ld_a0 = self.LD.n
        G("memset", self.ONEF[:], 1.0, incs=[(GP, 1)])
        vv = self.V[:].rearrange("p (t h c) -> p t h c", h=NH, c=65)
        G("memset", vv[:, :, :, 64], 1.0, incs=[(GP, 1)])
        self.sw_a0, self.gp_a0 = self.SW.n, self.GP.n

    def phase_a1(self):
        T, Vx, PE, DV, SW, GP = self.T, self.Vx, self.PE, self.DV, self.SW, self.GP
        self.q_rope_done = []
        for t in range(NT):
            half = t % 2
            ps = self.PS_BIG[:, half * 1024:half * 1024 + 1024]
            w = [(SW, self.sw_a0), (GP, self.gp_a0), (self.LD, self.ld_a0)] if t == 0 else []
            if t >= 2:
                w = [(self.AC, self.q_rope_done[t - 2][0])]
            for c in range(8):
                for nh2 in range(2):
                    T("matmul", ps[:, nh2 * 512:nh2 * 512 + 512],
                      self.XT[:, c * 2048 + t * 128: c * 2048 + t * 128 + 128],
                      self.W1B[:, c * 1024 + nh2 * 512: c * 1024 + nh2 * 512 + 512],
                      start=(c == 0), stop=(c == 7),
                      waits=w if (c == 0 and nh2 == 0) else (),
                      incs=[(PE, 1)] if (c == 7 and nh2 == 1) else ())
            self.q_rope_done.append(self.rope(t, ps, self.COSQ, self.SINQ, self.QR))
        self.q_done_pe = PE.n

        for m in range(4):
            for j in range(4):
                w = [(DV, DV.n)] if (m, j) != (0, 0) else []
                for c in range(8):
                    T("matmul", self.PS_A[:, 0:512],
                      self.W2B[:, c * 512 + m * 128: c * 512 + m * 128 + 128],
                      self.XT[:, c * 2048 + j * 512: c * 2048 + j * 512 + 512],
                      start=(c == 0), stop=(c == 7),
                      waits=w if c == 0 else (),
                      incs=[(PE, 1)] if c == 7 else ())
                Vx("tensor_copy", self.KVT[:, m * 2048 + j * 512: m * 2048 + j * 512 + 512],
                   self.PS_A[:, 0:512], waits=[(PE, PE.n)], incs=[(DV, 1)])
        self.kv_done_dv = DV.n

    def phase_a2(self):
        T, Vx, G = self.T, self.Vx, self.G
        PE, DV, SW = self.PE, self.DV, self.SW
        w1v = self.W1B[:].rearrange("p (c n) -> p c n", n=1024)
        G("dma_start", out=w1v[:, 0:4, :],
          in_=self.wkuT_d.ap().rearrange("(c p) n -> p c n", p=128),
          waits=[(PE, self.q_done_pe)], incs=[(SW, 16)])
        G("dma_start", out=w1v[:, 4:8, :],
          in_=self.wvuT_d.ap().rearrange("(c p) n -> p c n", p=128), incs=[(SW, 16)])
        sw_ku = SW.n
        k_prev = None
        for t in range(NT):
            wk = [(SW, sw_ku), (DV, self.kv_done_dv)] if t == 0 else []
            if k_prev is not None:
                wk = [(self.AC, k_prev[0])]
            for c in range(4):
                for nh2 in range(2):
                    T("matmul", self.PS_BIG[:, nh2 * 512:nh2 * 512 + 512],
                      self.KVT[:, c * 2048 + t * 128: c * 2048 + t * 128 + 128],
                      self.W1B[:, c * 1024 + nh2 * 512: c * 1024 + nh2 * 512 + 512],
                      start=(c == 0), stop=(c == 3),
                      waits=wk if (c == 0 and nh2 == 0) else (),
                      incs=[(PE, 1)] if (c == 3 and nh2 == 1) else ())
            wv = [(DV, k_prev[1])] if k_prev is not None else []
            for c in range(4):
                for nh2 in range(2):
                    T("matmul", self.PS_BIG[:, 1024 + nh2 * 512:1024 + nh2 * 512 + 512],
                      self.KVT[:, c * 2048 + t * 128: c * 2048 + t * 128 + 128],
                      self.W1B[:, 4096 + c * 1024 + nh2 * 512: 4096 + c * 1024 + nh2 * 512 + 512],
                      start=(c == 0), stop=(c == 3),
                      waits=wv if (c == 0 and nh2 == 0) else (),
                      incs=[(PE, 1)] if (c == 3 and nh2 == 1) else ())
            vpe = PE.n
            k_ac, _ = self.rope(t, self.PS_BIG[:, 0:1024], self.COSK, self.SINK, self.KR)
            vdst = self.V[:].rearrange("p (tt h c) -> p tt h c", h=NH, c=65)[:, t, :, 0:64]
            Vx("tensor_copy", vdst, self.PS_BIG[:, 1024:2048].rearrange("p (h d) -> p h d", d=HD),
               waits=[(PE, vpe)], incs=[(DV, 1)])
            k_prev = (k_ac, DV.n)
        self.proj_done_dv = DV.n
        self.proj_done_pe = PE.n
        self.esr.close()

    def phase_attn(self):
        nc, T, Vx, A = self.nc, self.T, self.Vx, self.A
        PE, DV, AC = self.PE, self.DV, self.AC
        esr = self.esr = ExitStack()
        R = lambda name, shape, dt: esr.enter_context(
            nc.sbuf_tensor(name, shape, dt, side="right"))
        self.ATO = R("ATO", [128, 8 * 2048], BF16)
        esr2 = self.esr2 = ExitStack()
        R2 = lambda name, shape, dt: esr2.enter_context(
            nc.sbuf_tensor(name, shape, dt, side="right"))
        PTa = R2("PTa", [128, 1024], BF16)
        PTb = R2("PTb", [128, 1024], BF16)
        QRT = R2("QRT", [128, 2 * 2048], BF16)
        KRT = R2("KRT", [128, 2 * 2048], BF16)
        ZB = R2("ZB", [64, 1024], F32)
        NORMB = R2("NORMB", [128, 512], BF16)

        pv_consumed_dv = 0
        arena_free_ac = {0: 0, 1: 0}
        pt_free_pe = {}
        mask_dv = {}
        arena_war = {0: 0}
        tr_state = {}
        tr_done = {}

        def emit_tr(p2n, quarter):
            parn = p2n % 2
            ev = tr_state.setdefault(p2n, [])
            for bi, (srcbuf, dstbuf) in enumerate(((self.QR, QRT), (self.KR, KRT))):
                for t in range(NT):
                    jj = bi * NT + t
                    if jj // 8 != quarter:
                        continue
                    pbf = self.PS_BF[:, (jj % 2) * 1024:(jj % 2) * 1024 + 128]
                    w = []
                    if p2n == 0 and jj == 0:
                        w = [(DV, self.proj_done_dv)]
                    if len(ev) >= 2:
                        w.append((DV, ev[-2]))
                    T("transpose", pbf,
                      srcbuf[:, t * 1024 + p2n * 128: t * 1024 + p2n * 128 + 128],
                      self.IDENTBF[:], waits=w, incs=[(PE, 1)])
                    Vx("tensor_copy",
                       dstbuf[:, parn * 2048 + t * 128: parn * 2048 + (t + 1) * 128],
                       pbf, waits=[(PE, PE.n)], incs=[(DV, 1)])
                    ev.append(DV.n)
            if quarter == 3:
                tr_done[p2n] = DV.n

        for q in range(4):
            emit_tr(0, q)
        for p2 in range(8):
            ha, hb = 2 * p2, 2 * p2 + 1
            par2 = p2 % 2
            first_st = True
            for qb in range(4):
                nkt = 4 * (qb + 1)
                qc = slice(par2 * 2048 + qb * 512, par2 * 2048 + qb * 512 + 512)
                outp = (self.PS_A[0:65, 0:512], self.PS_A[0:65, 512:1024])
                exp_ac = {0: 0, 1: 0}
                for ktp in range(nkt // 2):
                    par = 0
                    for hi, base in ((0, 0), (1, 64)):
                        PT = PTa if hi == 0 else PTb
                        PTp = PT[:, 0:1024]
                        arena = self.PS_BIG[:, hi * 1024:(hi + 1) * 1024]
                        wv = []
                        if first_st:
                            wv.append((DV, tr_done[p2]))
                            first_st = False
                        if arena_free_ac[hi]:
                            wv.append((AC, arena_free_ac[hi]))
                        if hi == 0 and ktp == 0 and arena_war[0]:
                            wv.append((DV, arena_war[0]))
                        for kk in range(2):
                            kt = 2 * ktp + kk
                            T("matmul", arena[:, kk * 512:kk * 512 + 512],
                              KRT[base:base + 64,
                                  par2 * 2048 + kt * 128: par2 * 2048 + (kt + 1) * 128],
                              QRT[base:base + 64, qc],
                              start=True, stop=True, tile_position=(base, 0),
                              waits=wv if kk == 0 else (), incs=[(PE, 1)])
                        st_pe = PE.n
                        wexp = [(PE, st_pe)]
                        if (hi, par) in pt_free_pe:
                            wexp.append((PE, pt_free_pe[(hi, par)]))
                        if (hi, par) in mask_dv:
                            wexp.append((DV, mask_dv[(hi, par)]))
                        A("activation", PTp[:, 0:1024], arena[:, 0:1024], AF.Exp,
                          bias=self.NEGSH[:, 0:1], scale=1.0, waits=wexp, incs=[(AC, 1)])
                        exp_ac[hi] = arena_free_ac[hi] = AC.n
                        for kk in range(2):
                            kt = 2 * ktp + kk
                            jj = kt - 4 * qb
                            if 0 <= jj <= 3:
                                Vx("tensor_tensor", PTp[:, kk * 512:(kk + 1) * 512],
                                   PTp[:, kk * 512:(kk + 1) * 512],
                                   self.MASK01[:, jj * 512:(jj + 1) * 512],
                                   ALU.mult, waits=[(AC, AC.n)], incs=[(DV, 1)])
                                mask_dv[(hi, par)] = DV.n
                    for hi in (0, 1):
                        PT = PTa if hi == 0 else PTb
                        PTp = PT[:, 0:1024]
                        h = ha if hi == 0 else hb
                        wpv = [(AC, exp_ac[hi])]
                        if (hi, par) in mask_dv:
                            wpv.append((DV, mask_dv[(hi, par)]))
                        if ktp == 0 and pv_consumed_dv:
                            wpv.append((DV, pv_consumed_dv))
                        for kk in range(2):
                            kt = 2 * ktp + kk
                            vsl = self.V[:, kt * NH * 65 + h * 65: kt * NH * 65 + h * 65 + 65]
                            T("matmul", outp[hi], vsl, PTp[:, kk * 512:(kk + 1) * 512],
                              start=(ktp == 0 and kk == 0),
                              stop=(ktp == nkt // 2 - 1 and kk == 1),
                              waits=wpv if kk == 0 else (), incs=[(PE, 1)])
                        pt_free_pe[(hi, par)] = PE.n
                pv_done_pe = PE.n
                A("activation", self.ZROW[64:65, 0:1024], self.PS_A[64:65, 0:1024],
                  AF.Copy, waits=[(PE, pv_done_pe)], incs=[(AC, 1)])
                T("matmul", self.PS_BIG[0:64, 0:512], self.ONEF[64:65, 0:64],
                  self.ZROW[64:65, 0:512], start=True, stop=True,
                  waits=[(AC, AC.n), (AC, arena_free_ac[0])], incs=[(PE, 1)])
                T("matmul", self.PS_BIG[0:64, 512:1024], self.ONEF[64:65, 0:64],
                  self.ZROW[64:65, 512:1024], start=True, stop=True, incs=[(PE, 1)])
                Vx("reciprocal", ZB[0:64, 0:512], self.PS_BIG[0:64, 0:512],
                   waits=[(PE, PE.n)], incs=[(DV, 1)])
                zb_a = DV.n
                Vx("reciprocal", ZB[0:64, 512:1024],
                   self.PS_BIG[0:64, 512:1024], incs=[(DV, 1)])
                zb_b = DV.n
                ocol = p2 * 2048 + qb * 512
                Vx("tensor_tensor", self.ATO[0:64, ocol:ocol + 512],
                   self.PS_A[0:64, 0:512], ZB[0:64, 0:512], ALU.mult,
                   waits=[(DV, zb_a)], incs=[(DV, 1)])
                Vx("tensor_tensor", NORMB[0:64, 0:512],
                   self.PS_A[0:64, 512:1024], ZB[0:64, 512:1024], ALU.mult,
                   waits=[(DV, zb_b)], incs=[(DV, 1)])
                pv_consumed_dv = DV.n
                T("matmul", self.PS_BIG[:, 0:512], self.SHIFT64[0:64, :],
                  NORMB[0:64, 0:512], start=True, stop=True,
                  waits=[(DV, DV.n)], incs=[(PE, 1)])
                Vx("tensor_copy", self.ATO[64:128, ocol:ocol + 512],
                   self.PS_BIG[64:128, 0:512], waits=[(PE, PE.n)], incs=[(DV, 1)])
                arena_war[0] = DV.n
                if p2 < 7:
                    emit_tr(p2 + 1, qb)
        self.attn_done_dv = DV.n
        esr2.close()

    def phase_wo_ln(self):
        nc, T, Vx, A, G, Y = self.nc, self.T, self.Vx, self.A, self.G, self.Y
        PE, DV, AC, SW, LD, ST = self.PE, self.DV, self.AC, self.SW, self.LD, self.ST
        self.esqkv.close()
        self.X1T = self.esl.enter_context(
            nc.sbuf_tensor("X1T", [128, 8 * 2048], BF16, side="left"))
        esr2 = ExitStack()
        R2 = lambda name, shape, dt: esr2.enter_context(
            nc.sbuf_tensor(name, shape, dt, side="right"))
        XF = R2("XF", [128, 2 * 1024], F32)
        XA = R2("XA", [128, 4 * 1024], F32)
        MV = R2("MV", [128, 16], F32)

        w1v = self.W1B[:].rearrange("p (c n) -> p c n", n=1024)
        G("dma_start", out=w1v[:, :, :],
          in_=self.woT_d.ap().rearrange("(c p) n -> p c n", p=128),
          waits=[(PE, self.PE.n)], incs=[(SW, 16)])
        sw_wo = SW.n

        x1_tap = self.tap("x1", [S, D])
        xa_free = None
        for tb in range(4):
            xa_ready = []
            for ch in range(8):
                ps = self.PS_BIG[:, (ch % 2) * 512:(ch % 2) * 512 + 512]
                w = [(SW, sw_wo), (DV, self.attn_done_dv)] if (tb == 0 and ch == 0) else []
                if tb > 0 or ch >= 2:
                    w = list(w) + [(AC, AC.n)]
                for kq in range(8):
                    T("matmul", ps,
                      self.W1B[:, kq * 1024 + ch * 128: kq * 1024 + ch * 128 + 128],
                      self.ATO[:, kq * 2048 + tb * 512: kq * 2048 + tb * 512 + 512],
                      start=(kq == 0), stop=(kq == 7),
                      waits=w if kq == 0 else (), incs=[(PE, 1)])
                A("copy", self.RS1[:, 0:512], ps, waits=[(PE, PE.n), (DV, DV.n)], incs=[(AC, 1)])
                ac_ev = AC.n
                for j in range(4):
                    pslot = (self.PS_BIG[:, 1024:1152] if j % 2 == 0
                             else self.PS_BIG[:, 1536:1664])
                    T("transpose", pslot, self.RS1[:, j * 128:(j + 1) * 128], self.IDENT[:],
                      waits=[(AC, ac_ev), (DV, DV.n)], incs=[(PE, 1)])
                    wxa = [(PE, PE.n)]
                    if ch == 0 and j == 0 and xa_free is not None:
                        wxa += [(SW, xa_free[0]), (self.STB, xa_free[1])]
                    Vx("tensor_copy", XA[:, j * 1024 + ch * 128: j * 1024 + ch * 128 + 128],
                       pslot, waits=wxa, incs=[(DV, 1)])
                xa_ready.append(DV.n)
            for j in range(4):
                t = tb * 4 + j
                xf = XF[:, (j % 2) * 1024:(j % 2) * 1024 + 1024]
                LDP = self.LDA if j % 2 == 0 else self.LDB
                Y("dma_start", out=xf, in_=self.x_d.ap()[t * 128:(t + 1) * 128, :],
                  waits=[(DV, DV.n)], incs=[(LDP, 16)])
                xa = XA[:, j * 1024:(j + 1) * 1024]
                Vx("tensor_add", xa, xa, xf, waits=[(LDP, LDP.n), (DV, xa_ready[-1])], incs=[(DV, 1)])
                Vx("bn_stats", MV[:, 0:6], xa[:, 0:512], waits=[(DV, DV.n)], incs=[(DV, 1)])
                Vx("bn_stats", MV[:, 6:12], xa[:, 512:1024], incs=[(DV, 1)])
                Vx("bn_aggr", MV[:, 12:14], MV[:, 0:12], waits=[(DV, DV.n)], incs=[(DV, 1)])
                Vx("tensor_scalar", MV[:, 14:15], MV[:, 13:14], 1e-5, None, ALU.add,
                   waits=[(DV, DV.n)], incs=[(DV, 1)])
                A("sqrt", MV[:, 15:16], MV[:, 14:15], waits=[(DV, DV.n)], incs=[(AC, 1)])
                Vx("reciprocal", MV[:, 10:11], MV[:, 15:16], waits=[(AC, AC.n)], incs=[(DV, 1)])
                Vx("tensor_scalar", xa, xa, MV[:, 12:13], MV[:, 10:11],
                   ALU.subtract, ALU.mult, waits=[(DV, DV.n)], incs=[(DV, 1)])
                x1dv = DV.n
                Y("dma_start", out=self.x1f_dram.ap()[t * 128:(t + 1) * 128, :], in_=xa,
                  waits=[(DV, x1dv)], incs=[(self.STB, 16)])
                if x1_tap is not None:
                    Y("dma_start", out=x1_tap.ap()[t * 128:(t + 1) * 128, :], in_=xa,
                      incs=[(ST, 16)])
                G("dma_start", out=self.cc_x1_in.ap()[t * 128:(t + 1) * 128, :], in_=xa,
                  waits=[(DV, x1dv)], incs=[(SW, 16)])
                for cch in range(8):
                    pslot = (self.PS_BIG[:, 1024:1152] if cch % 2 == 0
                             else self.PS_BIG[:, 1536:1664])
                    T("transpose", pslot, xa[:, cch * 128:(cch + 1) * 128], self.IDENT[:],
                      waits=[(DV, x1dv)] if cch == 0 else [(DV, DV.n)], incs=[(PE, 1)])
                    Vx("tensor_copy", self.X1T[:, cch * 2048 + t * 128: cch * 2048 + t * 128 + 128],
                       pslot, waits=[(PE, PE.n)], incs=[(DV, 1)])
            xa_free = (SW.n, self.STB.n)
        self.x1_done_dv = DV.n
        self.x1_store_sw = SW.n
        self.x1f_stb = self.STB.n
        esr2.close()
        self.esr.close()

    def finish_debug(self, phases):
        Y, ST, SW, DV, PE = self.Y, self.ST, self.SW, self.DV, self.PE
        G = self.G
        t_qr = self.tap("qr", [128, NT * 1024], BF16)
        if t_qr is not None and phases >= 1:
            G("dma_start", out=t_qr.ap(), in_=self.QR[:],
              waits=[(DV, DV.n)], incs=[(SW, 16)])
        t_kr = self.tap("kr", [128, NT * 1024], BF16)
        t_v = self.tap("v", [128, NT * NH * 65], BF16)
        if phases >= 2:
            if t_kr is not None:
                G("dma_start", out=t_kr.ap(), in_=self.KR[:],
                  waits=[(DV, DV.n)], incs=[(SW, 16)])
            if t_v is not None:
                G("dma_start", out=t_v.ap(), in_=self.V[:],
                  waits=[(DV, DV.n)], incs=[(SW, 16)])
        if phases >= 3:
            t_ato = self.tap("ato", [128, 8 * 2048], BF16)
            if t_ato is not None:
                G("dma_start", out=t_ato.ap(), in_=self.ATO[:],
                  waits=[(DV, DV.n)], incs=[(SW, 16)])
        Y("dma_start", out=self.out_d.ap(), in_=self.x1f_dram.ap(),
          waits=[(ST, ST.n)], incs=[(ST, 16)])
        Y("nop", waits=[(ST, ST.n), (SW, SW.n)])

    def finish_stub(self):
        Y, ST, SW = self.Y, self.ST, self.SW
        Y("dma_start", out=self.out_d.ap(), in_=self.x1f_dram.ap(),
          waits=[(ST, ST.n), (SW, SW.n)], incs=[(ST, 16)])
        Y("nop", waits=[(ST, ST.n), (SW, SW.n)])

    def phase_coll1(self):
        self.G("collective_compute", "AllGather", ALU.bypass,
               replica_groups=[list(range(NCORE))],
               ins=[self.cc_x1_in.ap()], outs=[self.cc_x1_out.ap()],
               waits=[(self.SW, self.x1_store_sw)], incs=[(self.CCS, 1)])
        self.cc1 = self.CCS.n

    def phase_router(self):
        nc, T, Vx, A, G, Y = self.nc, self.T, self.Vx, self.A, self.G, self.Y
        PE, DV, AC, SW, ST = self.PE, self.DV, self.AC, self.SW, self.ST
        esr = self.esrR = ExitStack()
        R = lambda name, shape, dt=F32: esr.enter_context(
            nc.sbuf_tensor(name, shape, dt, side="right"))
        GWB = R("GWB", [128, 8 * 64], BF16)
        IOTA64B = R("IOTA64B", [128, 64], F32)
        IOTA128B = R("IOTA128B", [128, 128], F32)
        TIB = R("TIB", [128, NT_ALL], F32)
        UB = R("UB", [128, 128], BF16)
        SEL8B = R("SEL8B", [128, EPC], BF16)
        PRIORB = R("PRIORB", [128, 128], F32)
        LOG = R("LOG", [128, 64], F32)
        TOP8 = R("TOP8", [128, 8], F32)
        IDXU = R("IDXU", [128, 8], U32)
        NEG = R("NEG", [128, 1], F32)
        E8 = R("E8", [128, 8], F32)
        DEN = R("DEN", [128, 2], F32)
        A0a = R("A0a", [128, 64], F32)
        A0b = R("A0b", [128, 64], F32)
        MK = R("MK", [128, 64], F32)
        A0T = R("A0T", [64, S], F32)
        ROWB = R("ROWB", [128, E], F32)
        self.ROWB = ROWB
        esl = self.esl
        Lf = lambda name, shape, dt: esl.enter_context(
            nc.sbuf_tensor(name, shape, dt, side="left"))
        self.IDXF = Lf("IDXF", [128, NT * 8], F32)
        self.A8 = Lf("A8", [128, NT * 8], F32)
        self.M_OWN = Lf("M_OWN", [128, NT * 64], BF16)
        self.SLOTS_OWN = Lf("SLOTS_OWN", [128, NT * 64], F32)
        self.FU = Lf("FU", [128, NT * 8], U32)
        self.VALID = Lf("VALID", [128, NT * 8], F32)
        self.TSS = Lf("TSS", [64, 128], F32)
        self.PROW = Lf("PROW", [128, 64], F32)

        SWR = self.SWR
        G("nop", waits=[(self.STB, self.x1f_stb), (SW, self.x1_store_sw),
                        (self.ST, self.ST.n)])
        G("dma_start", out=GWB[:].rearrange("p (c n) -> p c n", n=64),
          in_=self.gw_d.ap().rearrange("(c p) n -> p c n", p=128), incs=[(SWR, 16)])
        G("dma_start", out=IOTA64B[:], in_=self.iota64_c.ap(), incs=[(SWR, 16)])
        G("dma_start", out=IOTA128B[:], in_=self.iota128_c.ap(), incs=[(SWR, 16)])
        G("dma_start", out=TIB[:], in_=self.TI_c.ap(), incs=[(SWR, 16)])
        self.tib_swr = SWR.n
        G("dma_start", out=UB[:], in_=self.Ustrict_c.ap(), incs=[(SWR, 16)])
        G("dma_start", out=SEL8B[0:64, :], in_=self.sel8_d.ap(), incs=[(SWR, 16)])
        G("dma_start", out=PRIORB[0:64, :], in_=self.prior_d.ap(), incs=[(SWR, 16)])
        G("dma_start", out=ROWB[:], in_=self.rowbase_c.ap(), incs=[(SWR, 16)])
        sw_r = SWR.n

        for t in range(NT):
            w = [(SWR, sw_r), (DV, self.x1_done_dv)] if t == 0 else [(DV, DV.n)]
            for c in range(8):
                T("matmul", self.PS_A[:, 0:64],
                  self.X1T[:, c * 2048 + t * 128: c * 2048 + t * 128 + 128],
                  GWB[:, c * 64:(c + 1) * 64],
                  start=(c == 0), stop=(c == 7),
                  waits=w if c == 0 else (), incs=[(PE, 1)])
            Vx("tensor_copy", LOG[:], self.PS_A[:, 0:64], waits=[(PE, PE.n)], incs=[(DV, 1)])
            Vx("max", TOP8[:], LOG[:], waits=[(DV, DV.n)], incs=[(DV, 1)])
            Vx("max_index", IDXU[:], TOP8[:], LOG[:], waits=[(DV, DV.n)], incs=[(DV, 1)])
            Vx("tensor_copy", self.IDXF[:, t * 8:(t + 1) * 8], IDXU[:],
               waits=[(DV, DV.n)], incs=[(DV, 1)])
            Vx("tensor_scalar_mul", NEG[:], TOP8[:, 0:1], -1.0, incs=[(DV, 1)])
            A("activation", E8[:], TOP8[:], AF.Exp, bias=NEG[:, 0:1], scale=1.0,
              waits=[(DV, DV.n)], incs=[(AC, 1)])
            Vx("reduce_sum", DEN[:, 0:1], E8[:], axis=AX.X, waits=[(AC, AC.n)], incs=[(DV, 1)])
            Vx("reciprocal", DEN[:, 1:2], DEN[:, 0:1], waits=[(DV, DV.n)], incs=[(DV, 1)])
            Vx("tensor_scalar_mul", self.A8[:, t * 8:(t + 1) * 8], E8[:], DEN[:, 1:2],
               waits=[(DV, DV.n)], incs=[(DV, 1)])
            Vx("memset", A0a[:], 0.0, waits=[(PE, PE.n)], incs=[(DV, 1)])
            cur, nxt = A0a, A0b
            for kk in range(8):
                Vx("tensor_scalar", MK[:], IOTA64B[:], self.IDXF[:, t * 8 + kk: t * 8 + kk + 1],
                   None, ALU.is_equal, waits=[(DV, DV.n)], incs=[(DV, 1)])
                Vx("scalar_tensor_tensor", nxt[:], MK[:], self.A8[:, t * 8 + kk: t * 8 + kk + 1],
                   cur[:], ALU.mult, ALU.add, waits=[(DV, DV.n)], incs=[(DV, 1)])
                cur, nxt = nxt, cur
            Vx("tensor_single_scalar", self.M_OWN[:, t * 64:(t + 1) * 64], cur[:], 0.0,
               ALU.not_equal, waits=[(DV, DV.n)], incs=[(DV, 1)])
            T("transpose", self.PS_BIG[0:64, 1024:1152], cur[:],
              self.IDENT[:], waits=[(DV, DV.n), (AC, AC.n)], incs=[(PE, 1)])
            Vx("tensor_copy", A0T[0:64, t * 128:(t + 1) * 128], self.PS_BIG[0:64, 1024:1152],
               waits=[(PE, PE.n)], incs=[(DV, 1)])
        Y("dma_start", out=self.cc_at_in.ap(), in_=A0T[0:64, :],
          waits=[(DV, DV.n)], incs=[(self.STA, 16)])
        self.G("collective_compute", "AllGather", ALU.bypass,
               replica_groups=[list(range(NCORE))],
               ins=[self.cc_at_in.ap()], outs=[self.cc_at_out.ap()],
               waits=[(self.STA, self.STA.n)], incs=[(self.CCS, 1)])
        self.cc2 = self.CCS.n
        self.router_dv = DV.n
        self.GWB, self.IOTA64B, self.IOTA128B = GWB, IOTA64B, IOTA128B
        self.TIB, self.UB, self.SEL8B, self.PRIORB = TIB, UB, SEL8B, PRIORB

    def phase_shared(self):
        nc, T, Vx, A, G, Y = self.nc, self.T, self.Vx, self.A, self.G, self.Y
        PE, DV, AC, SW, ST, LD = self.PE, self.DV, self.AC, self.SW, self.ST, self.LD
        esr = ExitStack()
        R = lambda name, shape, dt: esr.enter_context(
            nc.sbuf_tensor(name, shape, dt, side="right"))
        SGW = R("SGW", [128, 4 * 1024], BF16)
        SDWB = R("SDWB", [128, 2 * 1024], BF16)
        SGTB = R("SGTB", [128, 8], BF16)
        SIG = R("SIG", [128, 16], F32)
        ACT_SH = R("ACT_SH", [128, 16 * 512], BF16)
        SSCR = R("SSCR", [128, 512], F32)
        SHST = R("SHST", [128, 1024], F32)

        G("dma_start", out=SGTB[:].rearrange("p (c n) -> p c n", n=1),
          in_=self.sgt_d.ap().rearrange("(c p) n -> p c n", p=128),
          allow_slow_non_contiguous=True, incs=[(self.SWR, 16)])
        sw_sg = self.SWR.n
        for t in range(NT):
            w = [(self.SWR, sw_sg), (DV, self.x1_done_dv)] if t == 0 else [(AC, AC.n)]
            for c in range(8):
                T("matmul", self.PS_A[:, 512:513],
                  self.X1T[:, c * 2048 + t * 128: c * 2048 + t * 128 + 128],
                  SGTB[:, c:c + 1],
                  start=(c == 0), stop=(c == 7),
                  waits=w if c == 0 else (), incs=[(PE, 1)])
            A("activation", SIG[:, t:t + 1], self.PS_A[:, 512:513], AF.Sigmoid,
              waits=[(PE, PE.n)], incs=[(AC, 1)])
        down_pe_prev = 0
        sh_st_prev = 0
        ld_marks = {}
        for tb in range(4):
            silu_ac = 0
            mult_dv = 0
            for fp in range(16):
                bofs = (fp % 2) * 2048
                wld = [(PE, ld_marks[fp % 2])] if (fp % 2) in ld_marks else []
                SWP = self.SWGA if fp % 2 == 0 else self.SWGB
                G("dma_start",
                  out=SGW[:, bofs:bofs + 1024].rearrange("p (c n) -> p c n", n=128),
                  in_=self.sgu_d.ap()[:, fp * 128:(fp + 1) * 128]
                      .rearrange("(c p) n -> p c n", p=128),
                  waits=wld, incs=[(SWP, 16)])
                G("dma_start",
                  out=SGW[:, bofs + 1024:bofs + 2048].rearrange("p (c n) -> p c n", n=128),
                  in_=self.sgu_d.ap()[:, (fp + 16) * 128:(fp + 17) * 128]
                      .rearrange("(c p) n -> p c n", p=128), incs=[(SWP, 16)])
                sw_f = SWP.n
                wmm = [(SWP, sw_f)]
                if silu_ac:
                    wmm.append((AC, silu_ac))
                if mult_dv:
                    wmm.append((DV, mult_dv))
                if tb > 0 and fp == 0:
                    wmm.append((PE, down_pe_prev))
                    wmm.append((AC, AC.n))
                    wmm.append((DV, DV.n))
                for half, pofs in ((0, 0), (1, 512)):
                    for c in range(8):
                        T("matmul", self.PS_BIG[:, pofs:pofs + 512],
                          SGW[:, bofs + half * 1024 + c * 128: bofs + half * 1024 + c * 128 + 128],
                          self.X1T[:, c * 2048 + tb * 512: c * 2048 + tb * 512 + 512],
                          start=(c == 0), stop=(c == 7),
                          waits=wmm if (half == 0 and c == 0) else (), incs=[(PE, 1)])
                ld_marks[fp % 2] = PE.n
                A("activation", SSCR[:], self.PS_BIG[:, 0:512], AF.Silu,
                  waits=[(PE, PE.n)], incs=[(AC, 1)])
                silu_ac = AC.n
                Vx("tensor_tensor", ACT_SH[:, fp * 512:(fp + 1) * 512],
                   SSCR[:], self.PS_BIG[:, 512:1024], ALU.mult,
                   waits=[(AC, silu_ac)], incs=[(DV, 1)])
                mult_dv = DV.n
            act_dv = DV.n
            pj = [self.PS_BIG[:, 0:1024], self.PS_BIG[:, 1024:2048],
                  self.PS_A[:, 0:1024], None]
            for fp in range(16):
                bofs = (fp % 2) * 1024
                wld = []
                if ("d", fp % 2) in ld_marks:
                    wld.append((PE, ld_marks[("d", fp % 2)]))
                SWP2 = self.SWDA if fp % 2 == 0 else self.SWDB
                G("dma_start",
                  out=SDWB[:, bofs:bofs + 1024],
                  in_=self.sdw_d.ap()[fp * 128:(fp + 1) * 128, :],
                  waits=wld, incs=[(SWP2, 16)])
                sw_d = SWP2.n
                for j in range(3):
                    wdn = [(SWP2, sw_d), (DV, act_dv)]
                    if fp == 0:
                        wdn.append((AC, AC.n))
                    for nh in range(2):
                        T("matmul", pj[j][:, nh * 512:nh * 512 + 512],
                          ACT_SH[:, fp * 512 + j * 128: fp * 512 + j * 128 + 128],
                          SDWB[:, bofs + nh * 512: bofs + nh * 512 + 512],
                          start=(fp == 0), stop=(fp == 15),
                          waits=wdn if (j == 0 and nh == 0) else (), incs=[(PE, 1)])
                ld_marks[("d", fp % 2)] = PE.n
            for j in range(3):
                t = tb * 4 + j
                A("activation", SHST[:, 0:512], pj[j][:, 0:512], AF.Copy,
                  scale=SIG[:, t:t + 1], waits=[(PE, PE.n), (self.STB, sh_st_prev)], incs=[(AC, 1)])
                A("activation", SHST[:, 512:1024], pj[j][:, 512:1024], AF.Copy,
                  scale=SIG[:, t:t + 1], incs=[(AC, 1)])
                Y("dma_start", out=self.shf_dram.ap()[t * 128:(t + 1) * 128, :],
                  in_=SHST[:], waits=[(AC, AC.n)], incs=[(self.STB, 16)])
                sh_st_prev = self.STB.n
            for fp in range(16):
                bofs = (fp % 2) * 1024
                wld = [(PE, ld_marks[("d", fp % 2)])]
                SWP2 = self.SWDA if fp % 2 == 0 else self.SWDB
                G("dma_start",
                  out=SDWB[:, bofs:bofs + 1024],
                  in_=self.sdw_d.ap()[fp * 128:(fp + 1) * 128, :],
                  waits=wld, incs=[(SWP2, 16)])
                sw_d = SWP2.n
                wdn = [(SWP2, sw_d), (DV, act_dv)]
                if fp == 0:
                    wdn.append((AC, AC.n))
                for nh in range(2):
                    T("matmul", self.PS_A[:, nh * 512:nh * 512 + 512],
                      ACT_SH[:, fp * 512 + 3 * 128: fp * 512 + 3 * 128 + 128],
                      SDWB[:, bofs + nh * 512: bofs + nh * 512 + 512],
                      start=(fp == 0), stop=(fp == 15),
                      waits=wdn if nh == 0 else (), incs=[(PE, 1)])
                ld_marks[("d", fp % 2)] = PE.n
            t = tb * 4 + 3
            A("activation", SHST[:, 0:512], self.PS_A[:, 0:512], AF.Copy,
              scale=SIG[:, t:t + 1], waits=[(PE, PE.n), (self.STB, sh_st_prev)], incs=[(AC, 1)])
            A("activation", SHST[:, 512:1024], self.PS_A[:, 512:1024], AF.Copy,
              scale=SIG[:, t:t + 1], incs=[(AC, 1)])
            Y("dma_start", out=self.shf_dram.ap()[t * 128:(t + 1) * 128, :],
              in_=SHST[:], waits=[(AC, AC.n)], incs=[(self.STB, 16)])
            sh_st_prev = self.STB.n
            down_pe_prev = PE.n
        self.shared_st = self.STB.n
        self.shared_ac = self.AC.n
        self.shared_dv = self.DV.n
        self.esrS = esr

    def phase_tables(self):
        nc, T, Vx, A, G, Y = self.nc, self.T, self.Vx, self.A, self.G, self.Y
        PE, DV, AC, SW, ST = self.PE, self.DV, self.AC, self.SW, self.ST
        esr = self.esrT = ExitStack()
        R = lambda name, shape, dt: esr.enter_context(
            nc.sbuf_tensor(name, shape, dt, side="right"))
        ATILE = R("ATILE", [64, 2 * 128], BF16)
        AMT = R("AMT", [8, 1024], F32)
        MT = R("MT", [8, 1024], F32)
        SCN = R("SCN", [8, 1024], F32)
        CARRY = R("CARRY", [8, 1], F32)
        MSCR = R("MSCR", [128, 8], F32)
        MKT = R("MKT", [64, 128], F32)
        self.SLM = R("SLM", [128, NT_ALL * 8], F32)
        self.AMN = R("AMN", [128, NT_ALL * 8], F32)
        CS = R("CS", [64, 128], F32)
        OCAR = R("OCAR", [128, 64], F32)
        CBS = R("CBS", [128, 64], F32)
        MK2 = R("MK2", [128, 64], F32)
        OH = R("OH", [128, 512], F32)
        NCH_SLM = (max(hi for _, hi in expert_windows()) + 7) // 8
        TAM = R("TAM", [128, NCH_SLM * 8 * 8 * 2], F32)
        for e in range(8):
            Y("dma_start",
              out=TAM[:, 0:NCH_SLM * 8 * 16]
                  .rearrange("p (t e x) -> p t (e x)", e=8, x=2)[:, :, e * 2:e * 2 + 1],
              in_=self.TIB[:, 0:NCH_SLM * 8].rearrange("p (t o) -> p t o", o=1),
              allow_slow_non_contiguous=True,
              waits=([(self.SWR, self.tib_swr)] if e == 0 else []),
              incs=[(self.LD, 16)])
        tam_ti_ld = self.LD.n

        Vx("memset", CARRY[:], 0.0, waits=[(self.CCS, self.cc2)], incs=[(DV, 1)])
        for ch in range(16):
            for tau in range(8):
                gt = ch * 8 + tau
                r, col = gt // 16, (gt % 16) * 128
                G("dma_start", out=ATILE[0:64, (tau % 2) * 128:(tau % 2) * 128 + 128],
                  in_=self.cc_at_out.ap()[r * 64:(r + 1) * 64, col:col + 128],
                  waits=[(PE, PE.n), (self.CCS, self.cc2)], incs=[(SW, 16)])
                if ch < NCH_SLM:
                    wsel = [(SW, SW.n), (DV, DV.n)]
                    if ch == 0 and tau == 0:
                        wsel += [(AC, self.shared_ac), (DV, self.shared_dv)]
                    T("matmul", self.PS_A[0:8, tau * 128:(tau + 1) * 128],
                      self.SEL8B[0:64, :], ATILE[0:64, (tau % 2) * 128:(tau % 2) * 128 + 128],
                      start=True, stop=True,
                      waits=wsel, incs=[(PE, 1)])
                Vx("tensor_single_scalar", MKT[:, :],
                   ATILE[0:64, (tau % 2) * 128:(tau % 2) * 128 + 128], 0.0,
                   ALU.not_equal, waits=[(SW, SW.n), (DV, DV.n)], incs=[(DV, 1)])
                Vx("reduce_sum", self.TSS[0:64, gt:gt + 1], MKT[:, :], axis=AX.X,
                   waits=[(DV, DV.n)], incs=[(DV, 1)])
            if ch >= NCH_SLM:
                continue
            Vx("tensor_copy", AMT[:], self.PS_A[0:8, 0:1024], waits=[(PE, PE.n)], incs=[(DV, 1)])
            Vx("tensor_single_scalar", MT[:], AMT[:], 0.0, ALU.not_equal,
               waits=[(DV, DV.n)], incs=[(DV, 1)])
            Vx("tensor_tensor_scan", SCN[:], MT[:], MT[:], CARRY[:, 0:1],
               ALU.add, ALU.max, waits=[(DV, DV.n)], incs=[(DV, 1)])
            Vx("tensor_copy", CARRY[:], SCN[:, 1023:1024], waits=[(DV, DV.n)], incs=[(DV, 1)])
            for tau in range(8):
                gt = ch * 8 + tau
                pofs = 1024 + (tau % 2) * 512
                T("transpose", self.PS_BIG[0:128, pofs:pofs + 8],
                  SCN[:, tau * 128:(tau + 1) * 128], self.IDENT[0:8, 0:8],
                  waits=[(DV, DV.n)], incs=[(PE, 1)])
                T("transpose", self.PS_BIG[0:128, pofs + 128:pofs + 136],
                  AMT[:, tau * 128:(tau + 1) * 128], self.IDENT[0:8, 0:8], incs=[(PE, 1)])
                Vx("tensor_copy", self.AMN[:, gt * 8:(gt + 1) * 8],
                   self.PS_BIG[:, pofs + 128:pofs + 136], waits=[(PE, PE.n)], incs=[(DV, 1)])
                Vx("tensor_copy",
                   TAM[:, gt * 16: gt * 16 + 16].rearrange("p (e x) -> p e x", x=2)[:, :, 1:2],
                   self.PS_BIG[:, pofs + 128:pofs + 136].rearrange("p (e o) -> p e o", o=1),
                   waits=[(DV, DV.n)], incs=[(DV, 1)])
                Vx("tensor_single_scalar", MSCR[:], self.AMN[:, gt * 8:(gt + 1) * 8], 0.0,
                   ALU.not_equal, waits=[(DV, DV.n)], incs=[(DV, 1)])
                Vx("scalar_tensor_tensor", self.SLM[:, gt * 8:(gt + 1) * 8],
                   MSCR[:], -(1.0 + 1048576.0), self.PS_BIG[:, pofs:pofs + 8],
                   ALU.mult, ALU.add, waits=[(DV, DV.n)], incs=[(DV, 1)])
                Vx("tensor_scalar", self.SLM[:, gt * 8:(gt + 1) * 8],
                   self.SLM[:, gt * 8:(gt + 1) * 8], 1048576.0, None, ALU.add,
                   waits=[(DV, DV.n)], incs=[(DV, 1)])
        Vx("tensor_tensor", CS[:, :], self.TSS[0:64, :],
           self.PRIORB[0:64, :], ALU.mult, waits=[(DV, DV.n)], incs=[(DV, 1)])
        Vx("reduce_sum", self.PROW[0:64, 0:1], CS[:, :], axis=AX.X,
           waits=[(DV, DV.n)], incs=[(DV, 1)])
        T("transpose", self.PS_BIG[0:1, 1024:1088], self.PROW[0:64, 0:1],
          self.IDENT[0:64, 0:64], waits=[(DV, DV.n), (PE, PE.n)], incs=[(PE, 1)])
        Vx("tensor_copy", OCAR[0:1, :], self.PS_BIG[0:1, 1024:1088],
           waits=[(PE, PE.n)], incs=[(DV, 1)])
        for j in range(NT):
            T("matmul", self.PS_A[:, 0:64], self.UB[:],
              self.M_OWN[:, j * 64:(j + 1) * 64],
              start=True, stop=True, waits=[(DV, DV.n)], incs=[(PE, 1)])
            T("matmul", self.PS_BIG[:, 1024:1088], self.ONEF[0:1, :],
              OCAR[0:1, :], start=True, stop=True, incs=[(PE, 1)])
            Vx("tensor_copy", CBS[:], self.PS_BIG[:, 1024:1088],
               waits=[(PE, PE.n)], incs=[(DV, 1)])
            Vx("tensor_tensor", self.SLOTS_OWN[:, j * 64:(j + 1) * 64],
               self.PS_A[:, 0:64], CBS[:], ALU.add, waits=[(DV, DV.n)], incs=[(DV, 1)])
            T("matmul", self.PS_A[0:1, 512:576], self.ONES[:, 0:1],
              self.M_OWN[:, j * 64:(j + 1) * 64],
              start=True, stop=True, waits=[(DV, DV.n)], incs=[(PE, 1)])
            Vx("tensor_tensor", OCAR[0:1, :], self.PS_A[0:1, 512:576], OCAR[0:1, :],
               ALU.add, waits=[(PE, PE.n), (DV, DV.n)], incs=[(DV, 1)])
        self.TBUF = R("TBUF", [128, EPC * 4], U32)
        self.ABUF = R("ABUF", [128, EPC * 4], F32)
        wins = expert_windows()
        evac_dv = {0: 0, 1: 0}
        oh_pe = {}
        tam_gated = False
        tcnt = 0
        self.tbl_dv_e = {}
        for e in range(EPC):
            for g in range(4):
                lo, hi = wins[g]
                bank = (e * 4 + g) % 2
                pofs = bank * 512
                first = True
                for tau in range(lo, hi):
                    par = tcnt % 4
                    tcnt += 1
                    OHp = OH[:, par * 128:(par + 1) * 128]
                    wv0 = [(DV, DV.n)]
                    if par in oh_pe:
                        wv0.append((PE, oh_pe[par]))
                    Vx("tensor_scalar", OHp, self.IOTA128B[:],
                       float(128 * g), self.SLM[:, tau * 8 + e: tau * 8 + e + 1],
                       ALU.add, ALU.is_equal, waits=wv0, incs=[(DV, 1)])
                    wmm = [(DV, DV.n)]
                    if first and evac_dv[bank]:
                        wmm.append((DV, evac_dv[bank]))
                    if not tam_gated:
                        wmm.append((self.LD, tam_ti_ld))
                        tam_gated = True
                    T("matmul", self.PS_A[:, pofs:pofs + 2], OHp,
                      TAM[:, (tau * 8 + e) * 2: (tau * 8 + e) * 2 + 2],
                      start=first, stop=(tau == hi - 1),
                      waits=wmm, incs=[(PE, 1)])
                    oh_pe[par] = PE.n
                    first = False
                Vx("tensor_copy", self.TBUF[:, e * 4 + g: e * 4 + g + 1],
                   self.PS_A[:, pofs:pofs + 1], waits=[(PE, PE.n)], incs=[(DV, 1)])
                Vx("tensor_copy", self.ABUF[:, e * 4 + g: e * 4 + g + 1],
                   self.PS_A[:, pofs + 1:pofs + 2], incs=[(DV, 1)])
                evac_dv[bank] = DV.n
                if g == 3:
                    self.tbl_dv_e[e] = DV.n
        FROWS = R("FROWS", [128, 64], F32)
        FF8 = R("FF8", [128, 8], F32)
        for j in range(NT):
            Vx("tensor_single_scalar", MK2[:], self.SLOTS_OWN[:, j * 64:(j + 1) * 64],
               float(CAP - 1), ALU.is_gt, waits=[(DV, DV.n)], incs=[(DV, 1)])
            Vx("scalar_tensor_tensor", FROWS[:], MK2[:], 2097152.0,
               self.SLOTS_OWN[:, j * 64:(j + 1) * 64], ALU.mult, ALU.add,
               waits=[(DV, DV.n)], incs=[(DV, 1)])
            Vx("tensor_add", FROWS[:], FROWS[:], self.ROWB[:], waits=[(DV, DV.n)], incs=[(DV, 1)])
            for kk in range(8):
                Vx("tensor_scalar", MK2[:], self.IOTA64B[:],
                   self.IDXF[:, j * 8 + kk: j * 8 + kk + 1], None, ALU.is_equal,
                   waits=[(DV, DV.n)], incs=[(DV, 1)])
                Vx("tensor_tensor", MK2[:], FROWS[:], MK2[:], ALU.mult,
                   waits=[(DV, DV.n)], incs=[(DV, 1)])
                Vx("reduce_sum", FF8[:, kk:kk + 1], MK2[:], axis=AX.X,
                   waits=[(DV, DV.n)], incs=[(DV, 1)])
            Vx("tensor_copy", self.FU[:, j * 8:(j + 1) * 8], FF8[:],
               waits=[(DV, DV.n)], incs=[(DV, 1)])
            Vx("tensor_single_scalar", self.VALID[:, j * 8:(j + 1) * 8],
               FF8[:], float(NCORE * EPC * CAP - 1), ALU.is_le, incs=[(DV, 1)])
        self.tables_dv = DV.n
        self.tables_pe_end = PE.n
        tp = self.tap("slm", [128, NT_ALL * 8])
        if tp is not None:
            self.Y("dma_start", out=tp.ap(), in_=self.SLM[:], waits=[(DV, DV.n)], incs=[(ST, 16)])
        tp = self.tap("amn", [128, NT_ALL * 8])
        if tp is not None:
            self.Y("dma_start", out=tp.ap(), in_=self.AMN[:], waits=[(DV, DV.n)], incs=[(ST, 16)])
        tp = self.tap("sown", [128, NT * 64])
        if tp is not None:
            self.Y("dma_start", out=tp.ap(), in_=self.SLOTS_OWN[:], waits=[(DV, DV.n)], incs=[(ST, 16)])
        tp = self.tap("tbuf", [128, EPC * 4], U32)
        if tp is not None:
            self.Y("dma_start", out=tp.ap(), in_=self.TBUF[:], waits=[(DV, DV.n)], incs=[(ST, 16)])
        tp = self.tap("abuf", [128, EPC * 4])
        if tp is not None:
            self.Y("dma_start", out=tp.ap(), in_=self.ABUF[:], waits=[(DV, DV.n)], incs=[(ST, 16)])
        tp = self.tap("fu", [128, NT * 8], U32)
        if tp is not None:
            self.Y("dma_start", out=tp.ap(), in_=self.FU[:], waits=[(DV, DV.n)], incs=[(ST, 16)])

    def phase_experts(self):
        nc, T, Vx, A, G, Y = self.nc, self.T, self.Vx, self.A, self.G, self.Y
        PE, DV, AC, SW, ST = self.PE, self.DV, self.AC, self.SW, self.ST
        esr = self.esrE = ExitStack()
        R = lambda name, shape, dt: esr.enter_context(
            nc.sbuf_tensor(name, shape, dt, side="right"))
        BUFR = R("BUFR", [128, 2 * 1024], BF16)
        BUFT = R("BUFT", [128, 8 * 512], BF16)
        ACTE = R("ACTE", [128, 4 * 512], BF16)
        SSCR2 = R("SSCR2", [128, 512], F32)
        OSTG = R("OSTG", [128, 1024], F32)

        self.bc_x1 = self.nc.gpsimd.alloc_register("bc_x1")
        self.bc_ob = self.nc.gpsimd.alloc_register("bc_ob")
        G("reg_mov", self.bc_x1, T_ALL - 1)
        G("reg_mov", self.bc_ob, NCORE * EPC * CAP - 1)
        gath_sw = {}
        evac_bf = {}
        ostg_st = 0
        self.cc3a = None
        CHUNK_AG3 = True
        for e in range(EPC):
            if e == 4 and CHUNK_AG3:
                self.G("collective_compute", "AllGather", ALU.bypass,
                       replica_groups=[list(range(NCORE))],
                       ins=[self.cc_ob_in.ap()[0:2048, :]],
                       outs=[self.cc_ob_out.ap()[0:16384, :]],
                       waits=[(self.SWE, self.SWE.n)], incs=[(self.CCS, 1)])
                self.cc3a = self.CCS.n
            wld = [(PE, PE.n)] if e > 0 else [(PE, self.proj_done_pe)]
            G("dma_start", out=self.W1B[:].rearrange("p (c n) -> p c n", n=1024),
              in_=self.egu_d.ap()[e].rearrange("(c p) n -> p c n", p=128),
              waits=wld, incs=[(self.SWE, 16)])
            G("dma_start", out=self.W2B[:].rearrange("p (c n) -> p c n", n=1024),
              in_=self.edw_d.ap()[e].rearrange("(c p) n -> p c n", p=128),
              incs=[(self.SWE, 16)])
            w_sw = self.SWE.n
            for st4 in range(4):
                SWF = self.SWFA if st4 % 2 == 0 else self.SWFB
                G("indirect_dma_start",
                  out=BUFR[:, (st4 % 2) * 1024:(st4 % 2) * 1024 + 1024],
                  out_offset=None,
                  in_=self.cc_x1_out.ap(),
                  in_offset=IndirectOffsetOnAxis(self.TBUF[:, e * 4 + st4: e * 4 + st4 + 1], 0),
                  bounds_check=self.bc_x1, oob_is_err=False,
                  waits=[(DV, self.tbl_dv_e[e]), (self.CCS, self.cc1),
                         (PE, gath_sw.get(st4 % 2, 0))],
                  incs=[(SWF, 16)])
                sw_g = SWF.n
                for g4 in range(2):
                    h = (st4 * 2 + g4) % 2
                    base_c = g4 * 4
                    w0 = [(SWF, sw_g)]
                    if evac_bf.get(h):
                        w0.append((DV, evac_bf[h]))
                    elif e == 0 and st4 == 0:
                        w0.append((DV, self.shared_dv))
                    for i in range(4):
                        c = base_c + i
                        T("transpose", self.PS_BF[:, h * 1024 + i * 128: h * 1024 + (i + 1) * 128],
                          BUFR[:, (st4 % 2) * 1024 + c * 128: (st4 % 2) * 1024 + c * 128 + 128],
                          self.IDENTBF[:],
                          waits=w0 if i == 0 else (), incs=[(PE, 1)] if i == 3 else ())
                    Vx("tensor_copy",
                       BUFT[:, base_c * 512:(base_c + 4) * 512]
                           .rearrange("p (c x) -> p c x", x=512)[:, :, st4 * 128: st4 * 128 + 128],
                       self.PS_BF[:, h * 1024: h * 1024 + 512]
                           .rearrange("p (c x) -> p c x", x=128),
                       waits=[(PE, PE.n)], incs=[(DV, 1)])
                    evac_bf[h] = DV.n
                gath_sw[st4 % 2] = PE.n
            buft_dv = DV.n
            silu_ac = 0
            mult_dv = 0
            for f in range(4):
                wmm = [(self.SWE, w_sw), (DV, buft_dv)]
                if silu_ac:
                    wmm += [(AC, silu_ac), (DV, mult_dv)]
                for half, fch in ((0, f), (1, f + 4)):
                    for c in range(8):
                        T("matmul", self.PS_BIG[:, half * 512:half * 512 + 512],
                          self.W1B[:, c * 1024 + fch * 128: c * 1024 + fch * 128 + 128],
                          BUFT[:, c * 512:(c + 1) * 512],
                          start=(c == 0), stop=(c == 7),
                          waits=wmm if (half == 0 and c == 0) else (), incs=[(PE, 1)])
                A("activation", SSCR2[:], self.PS_BIG[:, 0:512], AF.Silu,
                  waits=[(PE, PE.n)], incs=[(AC, 1)])
                silu_ac = AC.n
                Vx("tensor_tensor", ACTE[:, f * 512:(f + 1) * 512], SSCR2[:],
                   self.PS_BIG[:, 512:1024], ALU.mult, waits=[(AC, silu_ac)], incs=[(DV, 1)])
                mult_dv = DV.n
            for st4 in range(4):
                wdn = [(DV, mult_dv)]
                if True:
                    wdn.append((AC, AC.n))
                for c2 in range(4):
                    for nh in range(2):
                        T("matmul", self.PS_A[:, nh * 512:nh * 512 + 512],
                          ACTE[:, c2 * 512 + st4 * 128: c2 * 512 + st4 * 128 + 128],
                          self.W2B[:, c2 * 1024 + nh * 512: c2 * 1024 + nh * 512 + 512],
                          start=(c2 == 0), stop=(c2 == 3),
                          waits=wdn if (c2 == 0 and nh == 0) else (), incs=[(PE, 1)])
                A("activation", OSTG[:, 0:512], self.PS_A[:, 0:512], AF.Copy,
                  scale=self.ABUF[:, e * 4 + st4: e * 4 + st4 + 1],
                  waits=[(PE, PE.n), (self.SWE, ostg_st)], incs=[(AC, 1)])
                A("activation", OSTG[:, 512:1024], self.PS_A[:, 512:1024], AF.Copy,
                  scale=self.ABUF[:, e * 4 + st4: e * 4 + st4 + 1], incs=[(AC, 1)])
                G("dma_start", out=self.cc_ob_in.ap()[e * CAP + st4 * 128: e * CAP + st4 * 128 + 128, :],
                  in_=OSTG[:], waits=[(AC, AC.n)], incs=[(self.SWE, 16)])
                ostg_st = self.SWE.n
        tp = self.tap("obuf", [EPC * CAP, D], BF16)
        if tp is not None:
            G("dma_start", out=tp.ap(), in_=self.cc_ob_in.ap(),
              waits=[(self.SWE, self.SWE.n)], incs=[(self.SWE, 16)])
        w3b = [(self.SWE, self.SWE.n)]
        if self.cc3a is not None:
            w3b.append((self.CCS, self.cc3a))
            self.G("collective_compute", "AllGather", ALU.bypass,
                   replica_groups=[list(range(NCORE))],
                   ins=[self.cc_ob_in.ap()[2048:4096, :]],
                   outs=[self.cc_ob_out.ap()[16384:32768, :]],
                   waits=w3b, incs=[(self.CCS, 1)])
        else:
            self.G("collective_compute", "AllGather", ALU.bypass,
                   replica_groups=[list(range(NCORE))],
                   ins=[self.cc_ob_in.ap()],
                   outs=[self.cc_ob_out.ap()],
                   waits=w3b, incs=[(self.CCS, 1)])
        self.cc3 = self.CCS.n
        tp2 = self.tap("oball", [NCORE * EPC * CAP, D], BF16)
        if tp2 is not None:
            G("dma_start", out=tp2.ap(), in_=self.cc_ob_out.ap(),
              waits=[(self.CCS, self.cc3)], incs=[(self.SWE, 16)])

    @staticmethod
    def BUFT_slice(BUFT, c, st4):
        return BUFT[:, c * 512 + st4 * 128: c * 512 + st4 * 128 + 128]

    def phase_combine(self):
        nc, T, Vx, A, G, Y = self.nc, self.T, self.Vx, self.A, self.G, self.Y
        PE, DV, AC, SW, ST, LD, GP = self.PE, self.DV, self.AC, self.SW, self.ST, self.LD, self.GP
        self.esrE.close()
        self.esrT.close()
        self.esrS.close()
        self.esrR.close()
        esr = ExitStack()
        R = lambda name, shape, dt: esr.enter_context(
            nc.sbuf_tensor(name, shape, dt, side="right"))
        GT = R("GT", [128, 2 * 8 * 1024], BF16)
        YS2 = R("YS2", [128, 2 * 1024], F32)
        X1L = R("X1L", [128, 2 * 1024], F32)
        SHL = R("SHL", [128, 2 * 1024], F32)
        MV2 = R("MV2", [128, 16], F32)

        Vx("memset", GT[:], 0.0,
           waits=[(self.CCS, self.cc3), (PE, self.tables_pe_end),
                  (DV, self.tables_dv)], incs=[(DV, 1)])
        gt_init_dv = DV.n
        gt_cons = {}
        x1_cons = {}
        sh_cons = {}
        y2_cons = {}
        tp = self.tap("ycomb", [S, D])
        for j in range(NT):
            half = j % 2
            gts = GT[:, half * 8192:(half + 1) * 8192]
            y2 = YS2[:, half * 1024:half * 1024 + 1024]
            x1l = X1L[:, half * 1024:half * 1024 + 1024]
            shl = SHL[:, half * 1024:half * 1024 + 1024]
            SWH = self.SWHA if half == 0 else self.SWHB
            LDP = self.LDA if half == 0 else self.LDB
            wms = [(self.CCS, self.cc3), (DV, gt_cons.get(half, gt_init_dv))]
            if j < 2:
                wms.append((self.STB, self.shared_st))
            for kk in range(8):
                G("indirect_dma_start", out=gts[:, kk * 1024:(kk + 1) * 1024],
                  out_offset=None,
                  in_=self.cc_ob_out.ap(),
                  in_offset=IndirectOffsetOnAxis(self.FU[:, j * 8 + kk: j * 8 + kk + 1], 0),
                  bounds_check=self.bc_ob, oob_is_err=False,
                  waits=wms if kk == 0 else (), incs=[(SWH, 16)])
            sw_g = SWH.n
            wx1 = [(self.STB, self.shared_st), (DV, x1_cons.get(half, 0))]
            wsh = [(self.STB, self.shared_st), (DV, sh_cons.get(half, 0))]
            if j < 2:
                wx1 += [(self.CCS, self.cc3), (PE, self.tables_pe_end),
                        (DV, self.tables_dv)]
                wsh += [(self.CCS, self.cc3), (PE, self.tables_pe_end),
                        (DV, self.tables_dv)]
            Y("dma_start", out=x1l, in_=self.x1f_dram.ap()[j * 128:(j + 1) * 128, :],
              waits=wx1, incs=[(LDP, 16)])
            Y("dma_start", out=shl, in_=self.shf_dram.ap()[j * 128:(j + 1) * 128, :],
              waits=wsh, incs=[(LDP, 16)])
            ld_j = LDP.n
            Vx("tensor_scalar_mul", y2, gts[:, 0:1024],
               self.VALID[:, j * 8: j * 8 + 1],
               waits=[(SWH, sw_g), (ST, y2_cons.get(half, 0)), (DV, DV.n)],
               incs=[(DV, 1)])
            for kk in range(1, 8):
                Vx("scalar_tensor_tensor", y2, gts[:, kk * 1024:(kk + 1) * 1024],
                   self.VALID[:, j * 8 + kk: j * 8 + kk + 1], y2,
                   ALU.mult, ALU.add, waits=[(DV, DV.n)], incs=[(DV, 1)])
            if tp is not None:
                Y("dma_start", out=tp.ap()[j * 128:(j + 1) * 128, :], in_=y2,
                  waits=[(DV, DV.n)], incs=[(ST, 16)])
                self.q["vector"].append(
                    lambda En, st=ST.n: En.wait_ge(ST.h, st))
            Vx("tensor_add", y2, y2, x1l, waits=[(LDP, ld_j), (DV, DV.n)], incs=[(DV, 1)])
            gt_cons[half] = DV.n
            x1_cons[half] = DV.n
            Vx("tensor_add", y2, y2, shl, waits=[(DV, DV.n)], incs=[(DV, 1)])
            sh_cons[half] = DV.n
            tpt = self.tap("ytot", [S, D])
            if tpt is not None:
                Y("dma_start", out=tpt.ap()[j * 128:(j + 1) * 128, :], in_=y2,
                  waits=[(DV, DV.n)], incs=[(ST, 16)])
                self.q["vector"].append(
                    lambda En, st=ST.n: En.wait_ge(ST.h, st))
            Vx("bn_stats", MV2[:, 0:6], y2[:, 0:512], waits=[(DV, DV.n)], incs=[(DV, 1)])
            Vx("bn_stats", MV2[:, 6:12], y2[:, 512:1024], incs=[(DV, 1)])
            Vx("bn_aggr", MV2[:, 12:14], MV2[:, 0:12], waits=[(DV, DV.n)], incs=[(DV, 1)])
            Vx("tensor_scalar", MV2[:, 14:15], MV2[:, 13:14], 1e-5, None, ALU.add,
               waits=[(DV, DV.n)], incs=[(DV, 1)])
            A("sqrt", MV2[:, 15:16], MV2[:, 14:15], waits=[(DV, DV.n)], incs=[(AC, 1)])
            Vx("reciprocal", MV2[:, 10:11], MV2[:, 15:16], waits=[(AC, AC.n)], incs=[(DV, 1)])
            Vx("tensor_scalar", y2, y2, MV2[:, 12:13], MV2[:, 10:11],
               ALU.subtract, ALU.mult, waits=[(DV, DV.n)], incs=[(DV, 1)])
            Y("dma_start", out=self.out_d.ap()[j * 128:(j + 1) * 128, :], in_=y2,
              waits=[(DV, DV.n)], incs=[(ST, 16)])
            y2_cons[half] = ST.n
        Y("nop", waits=[(ST, ST.n), (SW, SW.n), (self.SWHA, self.SWHA.n),
                        (self.SWHB, self.SWHB.n), (self.LDA, self.LDA.n),
                        (self.LDB, self.LDB.n), (self.STB, self.STB.n)])


def make_inputs_per_core(ins):
    x = np.asarray(ins["L0_attn_x"], np.float32)
    wq = np.asarray(ins["L0_attn_wq"], np.float32)
    wkv = np.asarray(ins["L0_attn_w_kv_down"], np.float32)
    wku = np.asarray(ins["L0_attn_w_k_up"], np.float32)
    wvu = np.asarray(ins["L0_attn_w_v_up"], np.float32)
    wo = np.asarray(ins["L0_attn_wo"], np.float32)
    base = {
        "wqT": np.ascontiguousarray(wq.T),
        "wkvT": np.ascontiguousarray(wkv.T),
        "wkuT": np.ascontiguousarray(wku.T),
        "wvuT": np.ascontiguousarray(wvu.T),
        "woT": np.ascontiguousarray(wo.T),
        "sgu": np.ascontiguousarray(ins["L0_ffn_shared_gate_up_w"], ).astype(np.float32, copy=False),
        "sdw": np.ascontiguousarray(ins["L0_ffn_shared_down_w"]).astype(np.float32, copy=False),
        "sgt": np.ascontiguousarray(ins["L0_ffn_shared_gate_w"]).astype(np.float32, copy=False),
        "gw": np.ascontiguousarray(ins["L0_ffn_gate_w"]).astype(np.float32, copy=False),
    }
    egu = np.asarray(ins["L0_ffn_expert_gate_up_w"], np.float32)
    edw = np.asarray(ins["L0_ffn_expert_down_w"], np.float32)
    maps = []
    for c in range(NCORE):
        m = dict(base)
        m["x"] = np.ascontiguousarray(x[c])
        m["xT"] = np.ascontiguousarray(x[c].T)
        myexp = list(range(4 * c, 4 * c + 4)) + list(range(32 + 4 * c, 32 + 4 * c + 4))
        m["egu"] = np.ascontiguousarray(egu[myexp])
        m["edw"] = np.ascontiguousarray(edw[myexp])
        sel = np.zeros((E, EPC), np.float32)
        for j, e in enumerate(myexp):
            sel[e, j] = 1.0
        m["sel8"] = sel
        pri = np.zeros((E, 128), np.float32)
        pri[:, : c * NT] = 1.0
        m["prior"] = pri
        maps.append(m)
    return maps


def build_nc(debug_taps=(), phases=99):
    b = Builder(debug_taps=debug_taps, phases=phases)
    b.phase_a0()
    b.phase_a1()
    if phases >= 2:
        b.phase_a2()
    if phases >= 3:
        b.phase_attn()
    if phases >= 4 and phases < 5:
        b.phase_wo_ln()
        b.finish_stub()
    elif phases >= 5:
        b.phase_wo_ln()
        b.phase_coll1()
        b.phase_router()
        b.phase_shared()
        b.phase_tables()
        b.phase_experts()
        b.phase_combine()
    else:
        b.finish_debug(phases)
    b.emit_all()
    return b




def kernel(**inputs):
    b = build_nc(debug_taps=(), phases=9)
    maps = make_inputs_per_core(inputs)
    res = run_bass_kernel_spmd(b.nc, maps, core_ids=list(range(NCORE)))
    return np.stack([np.asarray(res.results[c]["out"], np.float32)
                     for c in range(NCORE)], axis=0)
